# version 16
# baseline (speedup 1.0000x reference)
import numpy as np
import ml_dtypes
import concourse.bass as bass
import concourse.mybir as mybir
from concourse.bass_utils import run_bass_kernel_spmd

F32 = mybir.dt.float32
BF16 = mybir.dt.bfloat16
AF = mybir.ActivationFunctionType
ALU = mybir.AluOpType
AX = mybir.AxisListType
BFNP = ml_dtypes.bfloat16

D = 2048
HEADS = 16
DH = 128
HF = 8192
NK = D // 128
NHC = HF // 128
SCALE = DH ** -0.5
EXP_C = 4.0
MASK_NEG = -1.0e9
LN_EPS = 1e-5
RMS_EPS = float(np.finfo(np.float32).eps)
BIGK = 1.0e12
ENGS = ("sync", "scalar", "vector", "tensor", "gpsimd")


class Ctx:
    def __init__(self, nc):
        self.nc = nc
        self.streams = {k: [] for k in ENGS}
        self.sems = {}
        self.offs = {}
        base = nc.sbuf_base
        self.sb_off = (base + 31) & ~31
        self.sb_limit = base + nc.sbuf_bytes_remaining

    def sbuf(self, name, shape, dtype, at=None):
        nbytes = int(np.prod(shape[1:])) * (2 if dtype == BF16 else 4)
        if at is None:
            off = self.sb_off
            self.sb_off = (off + nbytes + 31) & ~31
            assert self.sb_off <= self.sb_limit, f"SBUF overflow at {name}"
        else:
            off = at
            assert off % 32 == 0, name
        t = self.nc.alloc_sbuf_tensor_at(name, list(shape), dtype, offset=off)
        self.offs[name] = off
        return t

    def sem(self, name):
        h = self.nc.alloc_semaphore(name)
        self.sems[name] = [h, 0]
        return name

    def val(self, name):
        return self.sems[name][1]

    def emit(self, eng, f):
        self.streams[eng].append(f)

    def wait(self, eng, name, v=None):
        h, cur = self.sems[name]
        v = cur if v is None else v
        if v <= 0:
            return
        self.emit(eng, lambda e, h=h, v=v: e.wait_ge(h, v))

    def op(self, eng, fn, sem=None, by=1, fence=False):
        if fence and sem is None:
            sem = {"scalar": "sSC", "vector": "sDV", "gpsimd": "sPL", "tensor": "sPE"}[eng]
        if sem is not None:
            rec = self.sems[sem]
            rec[1] += by
            h = rec[0]
            self.emit(eng, lambda e, fn=fn, h=h, by=by: fn(e).then_inc(h, by))
            if fence:
                self.emit(eng, lambda e, h=h, v=rec[1]: e.wait_ge(h, v))
            return rec[1]
        self.emit(eng, fn)
        return None

    def dma(self, eng, out, in_, sem):
        return self.op(eng, lambda e, o=out, i=in_: e.dma_start(o, i), sem=sem, by=16)

    def barrier(self):
        nc = self.nc
        streams, self.streams = self.streams, {k: [] for k in ENGS}
        if not any(streams.values()):
            return
        with nc.Block() as blk:
            for k in ENGS:
                fns = streams[k]
                if fns:
                    getattr(blk, k)(lambda e, fns=fns: [f(e) for f in fns])


def build_nc(S, stop_stage=None, ffn_tiles=None):
    TSEQ = S // 128
    assert TSEQ % 2 == 0
    NPASS = TSEQ // 2
    TPS = 2
    TPP = 2 * TPS
    TS = 128 * TPP
    TW = 128 * TPS

    nc = bass.Bass("TRN2", target_bir_lowering=False)

    x_io = nc.dram_tensor("x", [2, S, D], F32, kind="ExternalInput").ap()
    wq_io = nc.dram_tensor("wq", [D, D], F32, kind="ExternalInput").ap()
    wkv_io = nc.dram_tensor("wkv", [D, 2 * DH], F32, kind="ExternalInput").ap()
    wo_io = nc.dram_tensor("wo", [D, D], F32, kind="ExternalInput").ap()
    wgu_io = nc.dram_tensor("wgu", [2 * HF, D], F32, kind="ExternalInput").ap()
    wp_io = nc.dram_tensor("wp", [D, HF], F32, kind="ExternalInput").ap()
    cos_io = nc.dram_tensor("costab", [128, S], F32, kind="ExternalInput").ap()
    sina_io = nc.dram_tensor("sinatab", [128, S], F32, kind="ExternalInput").ap()
    mneg_io = nc.dram_tensor("maskneg", [128, 128], F32, kind="ExternalInput").ap()
    mnegb_io = nc.dram_tensor("masknegbf", [128, 512], BF16, kind="ExternalInput").ap()
    idb_io = nc.dram_tensor("identbf", [128, 128], BF16, kind="ExternalInput").ap()
    idf_io = nc.dram_tensor("identf", [128, 128], F32, kind="ExternalInput").ap()
    y_io = nc.dram_tensor("y", [2, S, D], F32, kind="ExternalOutput").ap()

    sc_wq = nc.dram_tensor("sc_wq", [NK, 128, D], BF16, kind="Internal").ap()
    sc_wo = nc.dram_tensor("sc_wo", [NK, 128, D], BF16, kind="Internal").ap()
    sc_wgu = nc.dram_tensor("sc_wgu", [2 * NHC, 128, D], BF16, kind="Internal").ap()
    sc_wp = nc.dram_tensor("sc_wp", [NK, 128, HF], BF16, kind="Internal").ap()

    ctx = Ctx(nc)

    PS = [nc.alloc_psum_tensor(f"ps{i}", [128, 512], F32) for i in range(4)]
    PTBF = nc.alloc_psum_tensor("ptbf", [128, 2048], BF16)
    PAUX = nc.alloc_psum_tensor("paux", [128, 512], F32)
    PT2 = nc.alloc_psum_tensor("pt2", [128, 512], F32)

    xr = [ctx.sbuf(f"xr{j}", [128, D], F32) for j in range(TPP)]
    xn_fm = ctx.sbuf("xn_fm", [128, TPP * D], BF16)
    h_buf = ctx.sbuf("h_buf", [128, NHC * TS], BF16)
    HB = ctx.offs["h_buf"]
    q_fm = ctx.sbuf("q_fm", [128, HEADS * TS], BF16, at=HB)
    attn_fm = ctx.sbuf("attn_fm", [128, HEADS * TS], BF16, at=HB + HEADS * TS * 2)
    dummy = ctx.sbuf("dummy", [128, D], F32, at=HB + 2 * HEADS * TS * 2)

    k_fm = [ctx.sbuf(f"k_fm{s}", [128, S], BF16) for s in range(2)]
    v_tm = [ctx.sbuf(f"v_tm{s}", [128, S], BF16) for s in range(2)]
    U = ctx.sbuf("U", [128, 2 * S], BF16)
    U2 = ctx.sbuf("U2", [128, 2 * S], BF16)
    UT = ctx.sbuf("UT", [128, 1024], BF16)
    xntm = ctx.sbuf("xntm", [128, 2 * D], BF16)
    cosb = ctx.sbuf("cosb", [128, TW], F32)
    sinab = ctx.sbuf("sinab", [128, TW], F32)
    masknegb = ctx.sbuf("masknegb", [128, 512], BF16)
    idb = ctx.sbuf("identb", [128, 128], BF16)
    idf = ctx.sbuf("identf32", [128, 128], F32)
    ones_bf = ctx.sbuf("ones_bf", [128, 16], BF16)
    wkv_sb = ctx.sbuf("wkv_sb", [128, NK * 256], BF16)
    wqo_buf = ctx.sbuf("wqo_buf", [128, 2 * D], BF16)
    wg_buf = ctx.sbuf("wg_buf", [128, 2 * D], BF16)
    wu_buf = ctx.sbuf("wu_buf", [128, 2 * D], BF16)
    wp_buf = ctx.sbuf("wp_buf", [128, HF // 2], BF16)
    stg = ctx.sbuf("stg", [128, 512], F32)
    stg2 = ctx.sbuf("stg2", [128, 1024], F32)
    hsq = ctx.sbuf("hsq", [128, 1024], BF16)
    sup = ctx.sbuf("sup", [128, 16], F32)
    su = ctx.sbuf("su", [128, 2], F32)
    ru = ctx.sbuf("ru", [128, 2], F32)
    ssb = ctx.sbuf("ssb", [128, TPP], F32)
    inv = ctx.sbuf("inv", [128, TPP], F32)
    stats = ctx.sbuf("stats", [128, 8 * TPP], F32)
    vstg = ctx.sbuf("vstg", [128, 512], BF16)
    cbank = ctx.sbuf("cbank", [128, 8], F32)

    pa = ctx.offs["xr0"]
    p_wf = [ctx.sbuf(f"p_wf{i}", [128, D], F32, at=pa + i * 8192) for i in range(2)]
    p_wb = [ctx.sbuf(f"p_wb{i}", [128, D], BF16, at=pa + 16384 + i * 4096) for i in range(2)]
    p_sg = [ctx.sbuf(f"p_sg{i}", [128, D], BF16, at=pa + 24576 + i * 4096) for i in range(2)]
    p_tq = [ctx.sbuf(f"p_tq{i}", [128, D], BF16, at=HB + i * 4096) for i in range(2)]
    p_zq = [ctx.sbuf(f"p_zq{i}", [128, D], BF16, at=HB + 8192 + i * 4096) for i in range(2)]
    p_wh = [ctx.sbuf(f"p_wh{i}", [128, D], BF16, at=HB + 16384 + i * 4096) for i in range(2)]
    p_ts = [ctx.sbuf(f"p_ts{i}", [128, D], BF16, at=HB + 24576 + i * 4096) for i in range(2)]
    p_gs = [ctx.sbuf(f"p_gs{i}", [128, 32], F32, at=HB + 32768 + i * 160) for i in range(2)]
    p_sh = [ctx.sbuf(f"p_sh{i}", [128, 32], BF16, at=HB + 33120 + i * 64) for i in range(2)]
    p_sf = [ctx.sbuf(f"p_sf{i}", [128, 32], BF16, at=HB + 33248 + i * 64) for i in range(2)]
    p_kvf = ctx.sbuf("p_kvf", [128, NK * 256], F32, at=HB + 33376 + 32)
    p_se = [ctx.sbuf(f"p_se{i}", [128, D], BF16, at=HB + 50176 + i * 4096) for i in range(2)]
    p_zz = [ctx.sbuf(f"p_zz{i}", [128, D], BF16, at=HB + 33408 + i * 4096) for i in range(2)]

    p_wf.append(ctx.sbuf("p_wf2", [128, D], F32, at=ctx.offs["wqo_buf"]))
    p_wb.append(ctx.sbuf("p_wb2", [128, D], BF16, at=ctx.offs["wg_buf"]))
    p_sg.append(ctx.sbuf("p_sg2", [128, D], BF16, at=ctx.offs["wg_buf"] + 4096))
    p_tq.append(ctx.sbuf("p_tq2", [128, D], BF16, at=ctx.offs["wu_buf"]))
    p_zq.append(ctx.sbuf("p_zq2", [128, D], BF16, at=ctx.offs["wu_buf"] + 4096))
    p_zz.append(ctx.sbuf("p_zz2", [128, D], BF16, at=ctx.offs["wp_buf"]))
    p_wh.append(ctx.sbuf("p_wh2", [128, D], BF16, at=ctx.offs["wp_buf"] + 4096))
    p_se.append(ctx.sbuf("p_se2", [128, D], BF16, at=ctx.offs["xn_fm"]))
    p_ts.append(ctx.sbuf("p_ts2", [128, D], BF16, at=ctx.offs["xn_fm"] + 4096))
    p_gs.append(ctx.sbuf("p_gs2", [128, 32], F32, at=ctx.offs["xn_fm"] + 8192))
    p_sf.append(ctx.sbuf("p_sf2", [128, 32], BF16, at=ctx.offs["xn_fm"] + 8352))

    sIN = ctx.sem("sIN")
    sIN2 = ctx.sem("sIN2")
    sXs = [ctx.sem(f"sX{i}") for i in range(4)]
    sWRs = [ctx.sem("sWR0"), ctx.sem("sWR1"), ctx.sem("sWR2")]
    sPs = [ctx.sem("sP0"), ctx.sem("sP1"), ctx.sem("sP2")]
    sWs = [ctx.sem("sW0"), ctx.sem("sW1")]
    sWp = [ctx.sem("sWp0"), ctx.sem("sWp1")]
    sPE = ctx.sem("sPE")
    sSC = ctx.sem("sSC")
    sDV = ctx.sem("sDV")
    sPL = ctx.sem("sPL")
    sOUT = ctx.sem("sOUT")

    def _finish_early0():
        ctx.barrier()
        ctx.dma("scalar", y_io[0, 0:128, :], xr[0][:], sOUT)
        ctx.wait("sync", sOUT)
        ctx.barrier()

    ctx.dma("scalar", masknegb[:], mnegb_io, sIN2)
    ctx.dma("scalar", idb[:], idb_io, sIN2)
    ctx.dma("scalar", idf[:], idf_io, sIN2)
    ctx.dma("scalar", p_kvf[:].rearrange("p (k f) -> p k f", f=256),
            wkv_io.rearrange("(k p) f -> p k f", p=128), sIN2)
    ctx.wait("vector", sIN2)
    ctx.op("vector", lambda e: e.memset(ones_bf[:], 1.0))
    ctx.op("vector", lambda e: e.memset(cbank[:, 0:1], LN_EPS))
    ctx.op("vector", lambda e: e.memset(cbank[:, 1:2], SCALE))
    ctx.op("vector", lambda e: e.memset(cbank[:, 2:3], -EXP_C))
    ctx.op("vector", lambda e: e.memset(cbank[:, 3:4], RMS_EPS))
    ctx.op("vector", lambda e: e.memset(cbank[:, 4:5], 1.0 / HF))
    wkv_mark = ctx.op("vector", lambda e: e.tensor_copy(wkv_sb[:], p_kvf[:]), sem=sDV, fence=True)

    cast_done, write_done = {}, {}
    if stop_stage == "P0":
        _finish_early0()
        return nc

    def prep_attn(w_io, sc, tbase):
        for k in range(NK):
            t = tbase + k
            slot = t % 2
            if t >= 2:
                ctx.wait("sync", sSC, cast_done[t - 2])
            ctx.dma("sync", p_wf[slot][:], w_io[k * 128:(k + 1) * 128, :], sPs[slot])
            ctx.wait("scalar", sPs[slot])
            if t >= 2:
                ctx.wait("scalar", sWRs[slot])
            cast_done[t] = ctx.op(
                "scalar", lambda e, s=slot: e.copy(p_wb[s][:], p_wf[s][:]), sem=sSC)
            ctx.wait("scalar", sSC, cast_done[t])
            for hh in range(2):
                dst = sc[hh * 8:(hh + 1) * 8, :, k * 128:(k + 1) * 128].rearrange("m p f -> p m f")
                srcv = p_wb[slot][:, hh * 1024:(hh + 1) * 1024].rearrange("p (m f) -> p m f", f=128)
                write_done[t] = ctx.dma("scalar", dst, srcv, sWRs[slot])

    prep_attn(wq_io, sc_wq, 0)
    if stop_stage == "P1a":
        _finish_early0()
        return nc
    prep_attn(wo_io, sc_wo, NK)
    ctx.barrier()
    if stop_stage == "P1":
        _finish_early0()
        return nc
    ctx.wait("scalar", sWRs[0])
    ctx.wait("scalar", sWRs[1])
    ctx.wait("sync", sSC)
    ctx.barrier()

    NT_G = 2 * NHC
    NT_P = NK * (HF // D)

    n_ffn = NT_G + NT_P if ffn_tiles is None else ffn_tiles
    m_cast, m_abs, m_red, m_stt1, m_stt2, m_max, m_mul = {}, {}, {}, {}, {}, {}, {}
    m_T, m_ev0, m_ev1, m_sx, m_tg = {}, {}, {}, {}, {}

    def prep_src(t):
        if t < NT_G:
            return wgu_io[t * 128:(t + 1) * 128, :]
        i = t - NT_G
        kd, hs = i // 4, i % 4
        return wp_io[kd * 128:(kd + 1) * 128, hs * D:(hs + 1) * D]

    def prep_front(t):
        slot = t % 3
        if t >= 3:
            ctx.wait("sync", sSC, m_cast[t - 3])
        ctx.dma("sync", p_wf[slot][:], prep_src(t), sPs[slot])
        ctx.wait("scalar", sPs[slot])
        if t >= 3:
            ctx.wait("scalar", sDV, m_stt1[t - 3])
            ctx.wait("scalar", sPL, m_mul[t - 3])
        m_cast[t] = ctx.op("scalar", lambda e, s=slot: e.copy(p_wb[s][:], p_wf[s][:]),
                           sem=sSC, fence=True)
        ctx.op("scalar", lambda e, s=slot: e.sign(p_sg[s][:], p_wb[s][:]), sem=sSC, fence=True)
        m_abs[t] = ctx.op("scalar", lambda e, s=slot: e.activation(p_tq[s][:], p_wb[s][:], AF.Abs),
                          sem=sSC)
        ctx.wait("vector", sSC, m_cast[t])
        m_red[t] = ctx.op("vector", lambda e, s=slot: e.reduce_sum(
            p_gs[s][:], p_wb[s][:].rearrange("p (g e) -> p g e", e=64), axis=AX.X,
            apply_absolute_value=True), sem=sDV, fence=True)
        m_sf = ctx.op("vector", lambda e, s=slot: e.tensor_scalar(
            p_sf[s][:], p_gs[s][:], 1.0 / 64.0, 1.0e-8, ALU.mult, ALU.max), sem=sDV, fence=True)
        ctx.wait("gpsimd", sDV, m_sf)
        if t >= 3:
            ctx.wait("gpsimd", sDV, m_stt2[t - 3])
        m_sx[t] = ctx.op("gpsimd", lambda e, s=slot: e.tensor_copy(
            p_se[s][:].rearrange("p (g e) -> p g e", e=64),
            p_sf[s][:].unsqueeze(-1).broadcast_to([128, 32, 64])), sem=sPL)

    def prep_quant(t):
        slot = t % 3
        if t <= 1:
            ctx.wait("vector", sDV, wkv_mark)
        ctx.wait("vector", sSC, m_abs[t])
        ctx.wait("vector", sPL, m_sx[t])
        if t >= 3:
            ctx.wait("vector", sPL, m_mul[t - 3])
        m_stt1[t] = ctx.op("vector", lambda e, s=slot: e.scalar_tensor_tensor(
            p_zq[s][:], p_tq[s][:], 2.0, p_se[s][:],
            ALU.mult, ALU.subtract), sem=sDV, fence=True)
        m_stt2[t] = ctx.op("vector", lambda e, s=slot: e.scalar_tensor_tensor(
            p_zz[s][:], p_zq[s][:], BIGK, p_se[s][:],
            ALU.mult, ALU.min), sem=sDV)
        ctx.wait("gpsimd", sDV, m_stt2[t])
        ctx.wait("gpsimd", sSC, m_abs[t])
        if t >= 3:
            ctx.wait("gpsimd", sPE, m_T[t - 3])
        m_max[t] = ctx.op("gpsimd", lambda e, s=slot: e.tensor_scalar(
            p_zq[s][:], p_zz[s][:], 0.0, None, ALU.max), sem=sPL, fence=True)
        m_mul[t] = ctx.op("gpsimd", lambda e, s=slot: e.tensor_mul(
            p_wh[s][:], p_zq[s][:], p_sg[s][:]), sem=sPL)

    def prep_tail(t):
        slot = t % 3
        ctx.wait("scalar", sPE, m_tg[t][0])
        if t >= 3:
            ctx.wait("scalar", sWRs[slot])
        m_ev0[t] = ctx.op("scalar", lambda e, s=slot: e.copy(
            p_ts[s][:, 0:1024], PTBF[:, 0:1024]), sem=sSC)
        ctx.wait("vector", sPE, m_tg[t][1])
        if t >= 3:
            ctx.wait("vector", sWRs[slot])
        m_ev1[t] = ctx.op("vector", lambda e, s=slot: e.tensor_copy(
            p_ts[s][:, 1024:2048], PTBF[:, 1024:2048]), sem=sDV)
        if t < NT_G:
            dstw = sc_wgu[t]
        else:
            i = t - NT_G
            kd, hs = i // 4, i % 4
            dstw = sc_wp[kd, :, hs * D:(hs + 1) * D]
        ctx.wait("scalar", sSC, m_ev0[t])
        ctx.wait("scalar", sDV, m_ev1[t])
        ctx.dma("scalar", dstw, p_ts[slot][:], sWRs[slot])

    def prep_T(t):
        slot = t % 3
        ctx.wait("tensor", sPL, m_mul[t])
        m_tg[t] = {}
        for g in range(2):
            if t >= 1:
                if g == 0:
                    ctx.wait("tensor", sSC, m_ev0.get(t - 1, 0))
                else:
                    ctx.wait("tensor", sDV, m_ev1.get(t - 1, 0))
            gm = None
            for kk in range(8):
                k = g * 8 + kk
                gm = ctx.op("tensor", lambda e, s=slot, g=g, k=k, kk=kk: e.transpose(
                    PTBF[:, g * 1024 + kk * 128:g * 1024 + (kk + 1) * 128],
                    p_wh[s][:, k * 128:(k + 1) * 128], idb[:]),
                    sem=sPE if kk == 7 else None)
            m_tg[t][g] = gm
        m_T[t] = m_tg[t][1]

    for t in range(n_ffn + 1):
        if t >= 1:
            prep_quant(t - 1)
        if t < n_ffn:
            prep_front(t)
        if t >= 2:
            prep_tail(t - 2)
        if t >= 1:
            prep_T(t - 1)
    if n_ffn >= 1:
        prep_tail(n_ffn - 1)


    if ffn_tiles is not None:
        _finish_early0()
        return nc
    ctx.barrier()
    ctx.wait("sync", sWRs[0])
    ctx.wait("sync", sWRs[1])
    ctx.wait("sync", sWRs[2])
    ctx.wait("sync", sIN)
    ctx.wait("sync", sIN2)
    ctx.barrier()

    def _finish_early():
        ctx.dma("scalar", y_io[0, 0:128, :], xr[0][:], sOUT)
        ctx.wait("sync", sOUT)
        ctx.barrier()

    if stop_stage == "P":
        _finish_early()
        return nc

    def fm_rhs(k):
        v = xn_fm[:].rearrange("p (j k f) -> p k j f", k=NK, f=128)
        return v[:, k]

    ptbf_g = {0: None, 1: None}

    def wait_ptbf(eng, half):
        m = ptbf_g[half]
        if m is not None:
            ctx.wait(eng, m[0], m[1])

    def ln_block(src_tiles):
        stm = {}
        for j in range(TPP):
            xt = src_tiles[j]
            st = stats[:, j * 8:(j + 1) * 8]
            ctx.wait("scalar", sXs[j])
            ctx.op("scalar", lambda e, x=xt, st=st: e.activation(
                dummy[:], x[:], AF.Identity, accum_out=st[:, 0:1]), sem=sSC, fence=True)
            stm[j] = ctx.op("scalar", lambda e, x=xt, st=st: e.activation(
                dummy[:], x[:], AF.Square, accum_out=st[:, 1:2]), sem=sSC, fence=True)
        tpd = {}
        ev_half = {0: 0, 1: 0}
        for j in range(TPP):
            xt = src_tiles[j]
            st = stats[:, j * 8:(j + 1) * 8]
            xsl = j % 2
            ctx.wait("vector", sSC, stm[j])
            ctx.op("vector", lambda e, st=st: e.tensor_scalar(
                st[:, 2:3], st[:, 0:1], 1.0 / D, None, ALU.mult), fence=True)
            ctx.op("vector", lambda e, st=st: e.tensor_scalar(
                st[:, 3:4], st[:, 1:2], 1.0 / D, None, ALU.mult), fence=True)
            ctx.op("vector", lambda e, st=st: e.tensor_tensor(
                st[:, 4:5], st[:, 2:3], st[:, 2:3], ALU.mult), fence=True)
            v1 = ctx.op("vector", lambda e, st=st: e.tensor_sub(
                st[:, 5:6], st[:, 3:4], st[:, 4:5]), sem=sDV, fence=True)
            ctx.wait("scalar", sDV, v1)
            s1 = ctx.op("scalar", lambda e, st=st: e.activation(
                st[:, 6:7], st[:, 5:6], AF.Sqrt, bias=cbank[:, 0:1]), sem=sSC)
            ctx.wait("vector", sSC, s1)
            ctx.op("vector", lambda e, st=st: e.reciprocal(st[:, 7:8], st[:, 6:7]), fence=True)
            v2 = ctx.op("vector", lambda e, st=st: e.scalar_tensor_tensor(
                st[:, 4:5], st[:, 2:3], -1.0, st[:, 7:8], ALU.mult, ALU.mult), sem=sDV,
                fence=True)
            ctx.wait("scalar", sDV, v2)
            if j >= 2:
                ctx.wait("scalar", sPE, tpd[j - 2])
            s2 = ctx.op("scalar", lambda e, x=xt, st=st, xs=xsl: e.activation(
                xntm[:, xs * D:(xs + 1) * D], x[:], AF.Identity,
                scale=st[:, 7:8], bias=st[:, 4:5]), sem=sSC)
            ctx.wait("tensor", sSC, s2)
            for g in range(2):
                wait_ptbf("tensor", g)
                for kk in range(8):
                    k = g * 8 + kk
                    tpd[j] = ctx.op("tensor", lambda e, g=g, k=k, kk=kk, xs=xsl: e.transpose(
                        PTBF[:, g * 1024 + kk * 128:g * 1024 + (kk + 1) * 128],
                        xntm[:, xs * D + k * 128:xs * D + (k + 1) * 128], idb[:]),
                        sem=sPE if kk == 7 else None)
                ctx.wait("vector", sPE, tpd[j])
                ev_half[g] = ctx.op("vector", lambda e, g=g, j=j: e.tensor_copy(
                    xn_fm[:, j * D + g * 1024:j * D + (g + 1) * 1024],
                    PTBF[:, g * 1024:(g + 1) * 1024]), sem=sDV)
                ptbf_g[g] = (sDV, ev_half[g])
        return max(ev_half[0], ev_half[1])

    out_count = {}
    for p in range(NPASS):
        tlist = [(s, p * TPS + jj) for s in range(2) for jj in range(TPS)]

        if p >= 1:
            ctx.wait("sync", sOUT, out_count[p - 1])
        for j, (s, ti) in enumerate(tlist):
            ctx.dma("sync", xr[j][:], x_io[s, ti * 128:(ti + 1) * 128, :], sXs[j])
        c0 = p * TW
        ctx.dma("scalar", cosb[:], cos_io[:, c0:c0 + TW], sIN2)
        ctx.dma("scalar", sinab[:], sina_io[:, c0:c0 + TW], sIN2)
        lnm1 = ln_block(xr)

        if stop_stage == "A":
            _finish_early()
            return nc
        ctx.wait("tensor", sDV, lnm1)
        for k in range(NK):
            ctx.op("tensor", lambda e, k=k, r=fm_rhs(k): e.matmul(
                PS[0][:], wkv_sb[:, k * 256:k * 256 + 128], r,
                start=(k == 0), stop=(k == NK - 1)),
                sem=sPE if k == NK - 1 else None)
        k_pe = ctx.val(sPE)
        for k in range(NK):
            ctx.op("tensor", lambda e, k=k, r=fm_rhs(k): e.matmul(
                PS[1][:], wkv_sb[:, k * 256 + 128:k * 256 + 256], r,
                start=(k == 0), stop=(k == NK - 1)),
                sem=sPE if k == NK - 1 else None)
        v_pe = ctx.val(sPE)
        ctx.wait("vector", sIN2)
        ctx.wait("vector", sPE, k_pe)
        for s in range(2):
            b = s * TW
            dstk = k_fm[s][:, c0:c0 + TW]
            ctx.op("vector", lambda e, b=b: e.tensor_mul(
                stg[:, 0:TW], PS[0][:, b:b + TW], cosb[:]), fence=True)
            ctx.op("vector", lambda e, b=b: e.tensor_mul(
                stg2[0:64, 0:TW], PS[0][64:128, b:b + TW], sinab[0:64, :]), fence=True)
            ctx.op("vector", lambda e, b=b: e.tensor_mul(
                stg2[64:128, 0:TW], PS[0][0:64, b:b + TW], sinab[64:128, :]), fence=True)
            krope_mark = ctx.op("vector", lambda e, d=dstk: e.tensor_add(
                d, stg[:, 0:TW], stg2[:, 0:TW]), sem=sDV, fence=True)
        ctx.wait("scalar", sPE, v_pe)
        vc = ctx.op("scalar", lambda e: e.copy(vstg[:, 0:TS], PS[1][:, 0:TS]), sem=sSC)
        ctx.wait("tensor", sSC, vc)
        vtd = {}
        for j, (s, ti) in enumerate(tlist):
            half = j % 2
            wait_ptbf("tensor", half)
            tmark = ctx.op("tensor", lambda e, h=half, j=j: e.transpose(
                PTBF[:, h * 1024:h * 1024 + 128], vstg[:, j * 128:(j + 1) * 128], idb[:]), sem=sPE)
            ctx.wait("vector", sPE, tmark)
            vtd[j] = ctx.op("vector", lambda e, h=half, s=s, ti=ti: e.tensor_copy(
                v_tm[s][:, ti * 128:(ti + 1) * 128], PTBF[:, h * 1024:h * 1024 + 128]), sem=sDV)
            ptbf_g[half] = (sDV, vtd[j])
        b_krope = krope_mark

        if stop_stage == "B":
            _finish_early()
            return nc
        ctx.dma("sync", wqo_buf[:, 0:D], sc_wq[0], sWs[0])
        ctx.dma("sync", wqo_buf[:, D:2 * D], sc_wq[1], sWs[1])
        rope_done = {}
        for mi in range(HEADS):
            slot = mi % 2
            ctx.wait("tensor", sWs[slot])
            if mi == 0:
                ctx.wait("tensor", sDV, b_krope)
            if mi == 1:
                ctx.wait("tensor", sSC, vc)
            if mi >= 2:
                ctx.wait("tensor", sDV, rope_done[mi - 2])
            mmv = None
            for k in range(NK):
                mmv = ctx.op("tensor", lambda e, sl=slot, k=k, r=fm_rhs(k): e.matmul(
                    PS[sl][:], wqo_buf[:, sl * D + k * 128:sl * D + (k + 1) * 128], r,
                    start=(k == 0), stop=(k == NK - 1)),
                    sem=sPE if k == NK - 1 else None)
            if mi + 2 < HEADS:
                ctx.wait("sync", sPE, mmv)
                ctx.dma("sync", wqo_buf[:, slot * D:(slot + 1) * D], sc_wq[mi + 2], sWs[slot])
            ctx.wait("vector", sPE, mmv)
            for s in range(2):
                b = s * TW
                ctx.op("vector", lambda e, sl=slot, b=b: e.tensor_mul(
                    stg[:, 0:TW], PS[sl][:, b:b + TW], cosb[:]), fence=True)
                ctx.op("vector", lambda e, sl=slot, b=b: e.tensor_mul(
                    stg2[0:64, 0:TW], PS[sl][64:128, b:b + TW], sinab[0:64, :]), fence=True)
                ctx.op("vector", lambda e, sl=slot, b=b: e.tensor_mul(
                    stg2[64:128, 0:TW], PS[sl][0:64, b:b + TW], sinab[64:128, :]), fence=True)
                rope_done[mi] = ctx.op("vector", lambda e, mi=mi, b=b: e.tensor_add(
                    q_fm[:, mi * TS + b:mi * TS + b + TW], stg[:, 0:TW], stg2[:, 0:TW]),
                    sem=sDV, fence=True)
        c_mm_last = ctx.val(sPE)

        if stop_stage == "C":
            _finish_early()
            return nc
        NU = len(tlist) * HEADS
        exp_last, un_mark, t_mark, av_mark, aev_mark = {}, {}, {}, {}, {}
        ut_av = {0: 0, 1: 0}
        ps_exp = {0: (sDV, rope_done[HEADS - 2]), 1: (sDV, rope_done[HEADS - 1]),
                  2: None, 3: None}
        gctr = [0]
        sctr = [2]

        def d_front(u):
            j, h = u // HEADS, u % HEADS
            s, ti = tlist[j]
            nk = ti + 1
            ngrp = (nk + 3) // 4
            par = u % 2
            qh = q_fm[:, h * TS + j * 128:h * TS + (j + 1) * 128]
            ctx.wait("tensor", sDV, rope_done[h])
            for g in range(ngrp):
                w = min(4, nk - 4 * g)
                bank = sctr[0] % 4
                sctr[0] += 1
                has_diag = (4 * g <= nk - 1 < 4 * g + w)
                if ps_exp[bank] is not None:
                    ctx.wait("tensor", ps_exp[bank][0], ps_exp[bank][1])
                mm = ctx.op("tensor", lambda e, b=bank, g=g, w=w, qh=qh, s=s: e.matmul(
                    PS[b][:, 0:w * 128], qh, k_fm[s][:, g * 512:g * 512 + w * 128],
                    start=True, stop=not has_diag),
                    sem=None if has_diag else sPE)
                if has_diag:
                    mm = ctx.op("tensor", lambda e, b=bank, w=w: e.matmul(
                        PS[b][:, 0:w * 128], idb[:], masknegb[:, (4 - w) * 128:512],
                        start=False, stop=True), sem=sPE)
                ctx.wait("scalar", sPE, mm)
                if g == 0 and u >= 2:
                    ctx.wait("scalar", sDV, un_mark[u - 2])
                exp_last[u] = ctx.op("scalar", lambda e, b=bank, g=g, w=w, par=par: e.activation(
                    U[:, par * S + g * 512:par * S + g * 512 + w * 128],
                    PS[b][:, 0:w * 128], AF.Exp,
                    bias=cbank[:, 2:3], scale=cbank[:, 1:2],
                    accum_out=sup[:, par * 8 + g:par * 8 + g + 1]), sem=sSC)
                ps_exp[bank] = (sSC, exp_last[u])

        def d_tail(u):
            j, h = u // HEADS, u % HEADS
            s, ti = tlist[j]
            nk = ti + 1
            ngrp = (nk + 3) // 4
            par = u % 2
            ctx.wait("vector", sSC, exp_last[u])
            ctx.op("vector", lambda e, par=par, n=ngrp: e.reduce_sum(
                su[:, par:par + 1], sup[:, par * 8:par * 8 + n], axis=AX.X), fence=True)
            ctx.op("vector", lambda e, par=par: e.reciprocal(
                ru[:, par:par + 1], su[:, par:par + 1]), fence=True)
            if u >= 2:
                ctx.wait("vector", sPE, t_mark[u - 2])
            un_mark[u] = ctx.op("vector", lambda e, par=par, nk=nk: e.tensor_scalar(
                U2[:, par * S:par * S + nk * 128], U[:, par * S:par * S + nk * 128],
                ru[:, par:par + 1], None, ALU.mult), sem=sDV)
            ctx.wait("tensor", sDV, un_mark[u])
            if u >= 2:
                ctx.wait("tensor", sSC, aev_mark[u - 2])
            for g in range(ngrp):
                w = min(4, nk - 4 * g)
                slot = gctr[0] % 2
                wait_ptbf("tensor", slot)
                tm = None
                for kk in range(w):
                    kt = g * 4 + kk
                    tm = ctx.op("tensor", lambda e, sl=slot, kt=kt, kk=kk, par=par: e.transpose(
                        PTBF[:, sl * 1024 + kk * 128:sl * 1024 + (kk + 1) * 128],
                        U2[:, par * S + kt * 128:par * S + (kt + 1) * 128], idb[:]),
                        sem=sPE if kk == w - 1 else None)
                t_mark[u] = tm
                eeng = "scalar" if gctr[0] % 2 == 0 else "vector"
                esem = sSC if gctr[0] % 2 == 0 else sDV
                ctx.wait(eeng, sPE, tm)
                ctx.wait(eeng, sPE, ut_av[slot])
                ev = ctx.op(eeng, lambda e, sl=slot, w=w: (
                    e.copy if hasattr(e, "copy") else e.tensor_copy)(
                    UT[:, sl * 512:sl * 512 + w * 128],
                    PTBF[:, sl * 1024:sl * 1024 + w * 128]), sem=esem)
                ptbf_g[slot] = (esem, ev)
                ctx.wait("tensor", esem, ev)
                for kk in range(w):
                    kt = g * 4 + kk
                    av_mark[u] = ctx.op("tensor", lambda e, sl=slot, kk=kk, kt=kt, nk=nk, s=s, par=par: e.matmul(
                        PAUX[:, par * 128:(par + 1) * 128],
                        v_tm[s][:, kt * 128:(kt + 1) * 128],
                        UT[:, sl * 512 + kk * 128:sl * 512 + (kk + 1) * 128],
                        start=(kt == 0), stop=(kt == nk - 1), skip_group_check=True),
                        sem=sPE if kk == w - 1 else None)
                ut_av[slot] = av_mark[u]
                gctr[0] += 1
            ctx.wait("scalar", sPE, av_mark[u])
            aev_mark[u] = ctx.op("scalar", lambda e, h=h, j=j, par=par: e.copy(
                attn_fm[:, h * TS + j * 128:h * TS + (j + 1) * 128],
                PAUX[:, par * 128:(par + 1) * 128]), sem=sSC)

        for u in range(NU + 1):
            if u < NU:
                d_front(u)
            if u >= 1:
                d_tail(u - 1)
        d_ps_exp = ps_exp
        aev_last = aev_mark[NU - 1]

        if stop_stage == "D":
            _finish_early()
            return nc
        ctx.wait("sync", sPE, c_mm_last)
        ctx.dma("sync", wqo_buf[:, 0:D], sc_wo[0], sWs[0])
        ctx.dma("sync", wqo_buf[:, D:2 * D], sc_wo[1], sWs[1])
        add_done, ev_done, tp_done = {}, {}, {}
        for mo in range(NK + 1):
            slot = mo % 2
            mmv = None
            if mo < NK:
                ctx.wait("tensor", sWs[slot])
                if mo == 0:
                    ctx.wait("tensor", sSC, aev_last)
                if mo < 2:
                    m = d_ps_exp[mo]
                    if m is not None:
                        ctx.wait("tensor", m[0], m[1])
                if mo >= 2:
                    ctx.wait("tensor", sSC, ev_done[mo - 2])
                for ki in range(NK):
                    mmv = ctx.op("tensor", lambda e, sl=slot, ki=ki: e.matmul(
                        PS[sl][:], wqo_buf[:, sl * D + ki * 128:sl * D + (ki + 1) * 128],
                        attn_fm[:, ki * TS:(ki + 1) * TS],
                        start=(ki == 0), stop=(ki == NK - 1)),
                        sem=sPE if ki == NK - 1 else None)
            if mo >= 1:
                pc = mo - 1
                pslot = pc % 2
                pps = PT2 if pslot == 0 else PAUX
                ctx.wait("tensor", sSC, ev_done[pc])
                if pc >= 2:
                    ctx.wait("tensor", sDV, add_done[pc - 2])
                for j in range(TPP):
                    tp_done[pc] = ctx.op("tensor", lambda e, j=j, ps2=pps, sl2=pslot: e.transpose(
                        ps2[:, j * 128:(j + 1) * 128],
                        stg2[:, sl2 * 512 + j * 128:sl2 * 512 + (j + 1) * 128], idf[:]),
                        sem=sPE if j == TPP - 1 else None)
                ctx.wait("vector", sPE, tp_done[pc])
                for j in range(TPP):
                    add_done[pc] = ctx.op("vector", lambda e, j=j, pc=pc, ps2=pps: e.tensor_add(
                        xr[j][:, pc * 128:(pc + 1) * 128], xr[j][:, pc * 128:(pc + 1) * 128],
                        ps2[:, j * 128:(j + 1) * 128]),
                        sem=sDV if j == TPP - 1 else None)
            if mo >= NK:
                break
            if mo + 2 < NK:
                ctx.wait("sync", sPE, mmv)
                ctx.dma("sync", wqo_buf[:, slot * D:(slot + 1) * D], sc_wo[mo + 2], sWs[slot])
            ctx.wait("scalar", sPE, mmv)
            if mo >= 2:
                ctx.wait("scalar", sDV, add_done[mo - 2])
            ev_done[mo] = ctx.op("scalar", lambda e, sl=slot: e.copy(
                stg2[:, sl * 512:(sl + 1) * 512], PS[sl][:]), sem=sSC)
        e_mm_last = ctx.val(sPE)
        e_ev_done = ev_done
        e_add_last = add_done[NK - 1]

        if stop_stage == "E":
            _finish_early()
            return nc
        ctx.wait("scalar", sDV, e_add_last)
        lnm2 = ln_block(xr)

        if stop_stage == "F":
            _finish_early()
            return nc
        ctx.wait("sync", sPE, e_mm_last)
        for hc in range(2):
            ctx.dma("sync", wg_buf[:, (hc % 2) * D:(hc % 2 + 1) * D], sc_wgu[hc], sWs[hc % 2])
            ctx.dma("sync", wu_buf[:, (hc % 2) * D:(hc % 2 + 1) * D], sc_wgu[NHC + hc], sWs[hc % 2])
        HH = HF // 4
        ctx.dma("sync", wp_buf[:, 0:HH], sc_wp[0, :, 0:HH], sWp[0])
        ctx.dma("sync", wp_buf[:, HH:2 * HH], sc_wp[0, :, HH:2 * HH], sWp[1])
        hv_done, ss_done, h2_done, g_sl1 = {}, {}, {}, {}
        for hc in range(NHC + 1):
            slot = hc % 2
            ps_g, ps_u = PS[2 * slot], PS[2 * slot + 1]
            mmv = None
            if hc < NHC:
                ctx.wait("tensor", sWs[slot])
                if hc == 0:
                    ctx.wait("tensor", sDV, lnm2)
                    ctx.wait("tensor", sSC, e_ev_done[NK - 1])
                if hc == 1:
                    for m in (d_ps_exp[2], d_ps_exp[3]):
                        if m is not None:
                            ctx.wait("tensor", m[0], m[1])
                if hc >= 2:
                    ctx.wait("tensor", sDV, hv_done[hc - 2])
                for k in range(NK):
                    ctx.op("tensor", lambda e, pg=ps_g, sl=slot, k=k, r=fm_rhs(k): e.matmul(
                        pg[:], wg_buf[:, sl * D + k * 128:sl * D + (k + 1) * 128], r,
                        start=(k == 0), stop=(k == NK - 1)))
                for k in range(NK):
                    mmv = ctx.op("tensor", lambda e, pu=ps_u, sl=slot, k=k, r=fm_rhs(k): e.matmul(
                        pu[:], wu_buf[:, sl * D + k * 128:sl * D + (k + 1) * 128], r,
                        start=(k == 0), stop=(k == NK - 1)),
                        sem=sPE if k == NK - 1 else None)
            if hc >= 1:
                pc = hc - 1
                if pc == 0:
                    ctx.wait("tensor", sDV, e_add_last)
                ctx.wait("tensor", sSC, h2_done[pc])
                for j in range(TPP):
                    ss_done[pc] = ctx.op("tensor", lambda e, j=j, pc=pc, hq=pc % 2: e.matmul(
                        PAUX[:, 128 + j:129 + j],
                        hsq[:, hq * 512 + j * 128:hq * 512 + (j + 1) * 128],
                        ones_bf[:, 0:1],
                        start=(pc == 0), stop=(pc == NHC - 1), skip_group_check=True),
                        sem=sPE if j == TPP - 1 else None)
            if hc >= NHC:
                break
            if hc + 2 < NHC:
                ctx.wait("sync", sPE, mmv)
                ctx.dma("sync", wg_buf[:, slot * D:(slot + 1) * D], sc_wgu[hc + 2], sWs[slot])
                ctx.dma("sync", wu_buf[:, slot * D:(slot + 1) * D], sc_wgu[NHC + hc + 2], sWs[slot])
            ctx.wait("scalar", sPE, mmv)
            if hc >= 2:
                ctx.wait("scalar", sDV, hv_done[hc - 2])
            sl1 = ctx.op("scalar", lambda e, pg=ps_g, sl=slot: e.activation(
                stg[:, 0:512] if sl == 0 else stg2[:, 0:512], pg[:], AF.Silu), sem=sSC)
            g_sl1[hc] = sl1
            ctx.wait("vector", sSC, sl1)
            hv_done[hc] = ctx.op("vector", lambda e, pu=ps_u, hc=hc, sl=slot: e.tensor_mul(
                h_buf[:, hc * TS:(hc + 1) * TS],
                stg[:, 0:512] if sl == 0 else stg2[:, 0:512], pu[:]), sem=sDV)
            ctx.wait("scalar", sDV, hv_done[hc])
            if hc >= 2:
                ctx.wait("scalar", sPE, ss_done[hc - 2])
            h2_done[hc] = ctx.op("scalar", lambda e, hc=hc, hq=slot: e.activation(
                hsq[:, hq * 512:(hq + 1) * 512], h_buf[:, hc * TS:(hc + 1) * TS],
                AF.Square), sem=sSC)
        g_ss_last = ss_done[NHC - 1]
        g_hv = hv_done

        if stop_stage == "G":
            _finish_early()
            return nc
        ctx.wait("vector", sPE, g_ss_last)
        hv = ctx.op("vector", lambda e: e.tensor_copy(ssb[:], PAUX[:, 128:128 + TPP]), sem=sDV)
        ctx.wait("scalar", sDV, hv)
        sh = ctx.op("scalar", lambda e: e.activation(
            stats[:, 0:TPP], ssb[:], AF.Sqrt, scale=cbank[:, 4:5], bias=cbank[:, 3:4]), sem=sSC)
        ctx.wait("vector", sSC, sh)
        inv_mark = ctx.op("vector", lambda e: e.reciprocal(inv[:], stats[:, 0:TPP]), sem=sDV)
        h_hv_mark = hv

        if stop_stage == "H":
            _finish_early()
            return nc
        NHH = NHC // 4
        NST = 4
        add_done, ev_done, tp_done = {}, {}, {}
        for mo in range(NK + 1):
            slot = mo % 2
            mmv = None
            if mo < NK:
                if mo >= 2:
                    ctx.wait("tensor", sSC, ev_done[mo - 2])
                if mo == 0:
                    ctx.wait("tensor", sSC, g_sl1[NHC - 2])
                    ctx.wait("tensor", sDV, g_hv[NHC - 2])
                if mo == 1:
                    ctx.wait("tensor", sSC, g_sl1[NHC - 1])
                    ctx.wait("tensor", sDV, g_hv[NHC - 1])
                for h in range(NST):
                    step = NST * mo + h
                    bslot = step % 2
                    ctx.wait("tensor", sWp[bslot])
                    for hk_l in range(NHH):
                        hk = h * NHH + hk_l
                        if mo == 0:
                            ctx.wait("tensor", sDV, g_hv[hk])
                        mmv = ctx.op("tensor", lambda e, sl=slot, bs=bslot, hk=hk, hk_l=hk_l: e.matmul(
                            PS[sl][:], wp_buf[:, bs * HH + hk_l * 128:bs * HH + (hk_l + 1) * 128],
                            h_buf[:, hk * TS:(hk + 1) * TS],
                            start=(hk == 0), stop=(hk == NHC - 1)),
                            sem=sPE if hk_l == NHH - 1 else None)
                    nstep = step + 2
                    if nstep < NST * NK:
                        nmo, nh = nstep // NST, nstep % NST
                        ctx.wait("sync", sPE, mmv)
                        ctx.dma("sync", wp_buf[:, bslot * HH:(bslot + 1) * HH],
                                sc_wp[nmo, :, nh * HH:(nh + 1) * HH], sWp[bslot])
            if mo >= 1:
                pc = mo - 1
                pslot = pc % 2
                pps = PT2 if pslot == 0 else PAUX
                ctx.wait("tensor", sSC, ev_done[pc])
                if pc == 0:
                    ctx.wait("tensor", sDV, e_add_last)
                if pc == 1:
                    ctx.wait("tensor", sDV, h_hv_mark)
                if pc >= 2:
                    ctx.wait("tensor", sDV, add_done[pc - 2])
                for j in range(TPP):
                    tp_done[pc] = ctx.op("tensor", lambda e, j=j, ps2=pps, sl2=pslot: e.transpose(
                        ps2[:, j * 128:(j + 1) * 128],
                        stg2[:, sl2 * 512 + j * 128:sl2 * 512 + (j + 1) * 128], idf[:]),
                        sem=sPE if j == TPP - 1 else None)
                ctx.wait("vector", sPE, tp_done[pc])
                if pc == 0:
                    ctx.wait("vector", sDV, inv_mark)
                for j in range(TPP):
                    add_done[pc] = ctx.op("vector", lambda e, j=j, pc=pc, ps2=pps: e.scalar_tensor_tensor(
                        xr[j][:, pc * 128:(pc + 1) * 128],
                        ps2[:, j * 128:(j + 1) * 128], inv[:, j:j + 1],
                        xr[j][:, pc * 128:(pc + 1) * 128],
                        ALU.mult, ALU.add),
                        sem=sDV if j == TPP - 1 else None)
            if mo >= NK:
                break
            ctx.wait("scalar", sPE, mmv)
            if mo >= 2:
                ctx.wait("scalar", sDV, add_done[mo - 2])
            ev_done[mo] = ctx.op("scalar", lambda e, sl=slot: e.copy(
                stg2[:, sl * 512:(sl + 1) * 512], PS[sl][:]), sem=sSC)
        i_add_last = add_done[NK - 1]

        if stop_stage == "I":
            _finish_early()
            return nc
        ctx.wait("scalar", sDV, i_add_last)
        for j, (s, ti) in enumerate(tlist):
            ctx.dma("scalar", y_io[s, ti * 128:(ti + 1) * 128, :], xr[j][:], sOUT)
        out_count[p] = ctx.val(sOUT)
        ctx.barrier()

    ctx.wait("sync", sOUT)
    ctx.barrier()
    return nc


_CACHE = {}


def _get_nc(S):
    if S not in _CACHE:
        _CACHE[S] = build_nc(S)
    return _CACHE[S]


def _tables(S):
    inv_freq = 1.0 / (10000.0 ** (np.arange(0, DH, 2, dtype=np.float64) / DH))
    pos = np.arange(S, dtype=np.float64)[:, None] * inv_freq[None, :]
    cos = np.cos(pos)
    sin = np.sin(pos)
    cos_t = np.concatenate([cos, cos], axis=1).T.astype(np.float32)
    sina_t = np.concatenate([-sin, sin], axis=1).T.astype(np.float32)
    return np.ascontiguousarray(cos_t), np.ascontiguousarray(sina_t)


def kernel(L0_attn_x, L0_attn_Wq, L0_attn_Wkv, L0_attn_Wout, L0_ffn_w_gate_up, L0_ffn_w_proj):
    x = np.ascontiguousarray(np.asarray(L0_attn_x, dtype=np.float32))
    wq = np.ascontiguousarray(np.asarray(L0_attn_Wq, dtype=np.float32))
    wkv = np.ascontiguousarray(np.asarray(L0_attn_Wkv, dtype=np.float32))
    wo = np.ascontiguousarray(np.asarray(L0_attn_Wout, dtype=np.float32))
    wgu = np.ascontiguousarray(np.asarray(L0_ffn_w_gate_up, dtype=np.float32))
    wp = np.ascontiguousarray(np.asarray(L0_ffn_w_proj, dtype=np.float32))
    B, S, _ = x.shape
    n_cores = 8
    per = B // n_cores
    cos_t, sina_t = _tables(S)
    mneg_np = np.triu(np.full((128, 128), MASK_NEG, dtype=np.float32), 1)
    mnegb_np = np.zeros((128, 512), dtype=BFNP)
    mnegb_np[:, 384:512] = mneg_np.astype(BFNP)
    idb_np = np.eye(128, dtype=BFNP)
    idf_np = np.eye(128, dtype=np.float32)
    nc = _get_nc(S)
    in_maps = []
    for c in range(n_cores):
        in_maps.append({
            "x": np.ascontiguousarray(x[c * per:(c + 1) * per]),
            "wq": wq, "wkv": wkv, "wo": wo, "wgu": wgu, "wp": wp,
            "costab": cos_t, "sinatab": sina_t,
            "maskneg": mneg_np, "masknegbf": mnegb_np, "identbf": idb_np, "identf": idf_np,
        })
    res = run_bass_kernel_spmd(nc, in_maps, core_ids=list(range(n_cores)))
    out = np.concatenate([res.results[c]["y"] for c in range(n_cores)], axis=0)
    return np.ascontiguousarray(out.astype(np.float32))


# version 17
# speedup vs baseline: 1.4934x; 1.4934x over previous
import numpy as np
import ml_dtypes
import concourse.bass as bass
import concourse.mybir as mybir
from concourse.bass_utils import run_bass_kernel_spmd

F32 = mybir.dt.float32
BF16 = mybir.dt.bfloat16
AF = mybir.ActivationFunctionType
ALU = mybir.AluOpType
AX = mybir.AxisListType
BFNP = ml_dtypes.bfloat16

D = 2048
HEADS = 16
DH = 128
HF = 8192
NK = D // 128
NHC = HF // 128
SCALE = DH ** -0.5
EXP_C = 4.0
MASK_NEG = -1.0e9
LN_EPS = 1e-5
RMS_EPS = float(np.finfo(np.float32).eps)
BIGK = 1.0e12
ENGS = ("sync", "scalar", "vector", "tensor", "gpsimd")


class Ctx:
    def __init__(self, nc):
        self.nc = nc
        self.streams = {k: [] for k in ENGS}
        self.sems = {}
        self.offs = {}
        base = nc.sbuf_base
        self.sb_off = (base + 31) & ~31
        self.sb_limit = base + nc.sbuf_bytes_remaining

    def sbuf(self, name, shape, dtype, at=None):
        nbytes = int(np.prod(shape[1:])) * (2 if dtype == BF16 else 4)
        if at is None:
            off = self.sb_off
            self.sb_off = (off + nbytes + 31) & ~31
            assert self.sb_off <= self.sb_limit, f"SBUF overflow at {name}"
        else:
            off = at
            assert off % 32 == 0, name
        t = self.nc.alloc_sbuf_tensor_at(name, list(shape), dtype, offset=off)
        self.offs[name] = off
        return t

    def sem(self, name):
        h = self.nc.alloc_semaphore(name)
        self.sems[name] = [h, 0]
        return name

    def val(self, name):
        return self.sems[name][1]

    def emit(self, eng, f):
        self.streams[eng].append(f)

    def wait(self, eng, name, v=None):
        h, cur = self.sems[name]
        v = cur if v is None else v
        if v <= 0:
            return
        self.emit(eng, lambda e, h=h, v=v: e.wait_ge(h, v))

    def op(self, eng, fn, sem=None, by=1, fence=False):
        if fence and sem is None:
            sem = {"scalar": "sSC", "vector": "sDV", "gpsimd": "sPL", "tensor": "sPE"}[eng]
        if sem is not None:
            rec = self.sems[sem]
            rec[1] += by
            h = rec[0]
            self.emit(eng, lambda e, fn=fn, h=h, by=by: fn(e).then_inc(h, by))
            if fence:
                self.emit(eng, lambda e, h=h, v=rec[1]: e.wait_ge(h, v))
            return rec[1]
        self.emit(eng, fn)
        return None

    def dma(self, eng, out, in_, sem):
        return self.op(eng, lambda e, o=out, i=in_: e.dma_start(o, i), sem=sem, by=16)

    def barrier(self):
        nc = self.nc
        streams, self.streams = self.streams, {k: [] for k in ENGS}
        if not any(streams.values()):
            return
        with nc.Block() as blk:
            for k in ENGS:
                fns = streams[k]
                if fns:
                    getattr(blk, k)(lambda e, fns=fns: [f(e) for f in fns])


def build_nc(S, stop_stage=None, ffn_tiles=None):
    TSEQ = S // 128
    assert TSEQ % 2 == 0
    NPASS = TSEQ // 2
    TPS = 2
    TPP = 2 * TPS
    TS = 128 * TPP
    TW = 128 * TPS

    nc = bass.Bass("TRN2", target_bir_lowering=False)

    x_io = nc.dram_tensor("x", [2, S, D], F32, kind="ExternalInput").ap()
    wq_io = nc.dram_tensor("wq", [D, D], F32, kind="ExternalInput").ap()
    wkv_io = nc.dram_tensor("wkv", [D, 2 * DH], F32, kind="ExternalInput").ap()
    wo_io = nc.dram_tensor("wo", [D, D], F32, kind="ExternalInput").ap()
    wgu_io = nc.dram_tensor("wgu", [2 * HF, D], F32, kind="ExternalInput").ap()
    wp_io = nc.dram_tensor("wp", [D, HF], F32, kind="ExternalInput").ap()
    cos_io = nc.dram_tensor("costab", [128, S], F32, kind="ExternalInput").ap()
    sina_io = nc.dram_tensor("sinatab", [128, S], F32, kind="ExternalInput").ap()
    mneg_io = nc.dram_tensor("maskneg", [128, 128], F32, kind="ExternalInput").ap()
    mnegb_io = nc.dram_tensor("masknegbf", [128, 512], BF16, kind="ExternalInput").ap()
    idb_io = nc.dram_tensor("identbf", [128, 128], BF16, kind="ExternalInput").ap()
    idf_io = nc.dram_tensor("identf", [128, 128], F32, kind="ExternalInput").ap()
    y_io = nc.dram_tensor("y", [2, S, D], F32, kind="ExternalOutput").ap()

    sc_wq = nc.dram_tensor("sc_wq", [NK, 128, D], BF16, kind="Internal").ap()
    sc_wo = nc.dram_tensor("sc_wo", [NK, 128, D], BF16, kind="Internal").ap()
    sc_wgu = nc.dram_tensor("sc_wgu", [2 * NHC, 128, D], BF16, kind="Internal").ap()
    sc_wp = nc.dram_tensor("sc_wp", [NK, 128, HF], BF16, kind="Internal").ap()

    ctx = Ctx(nc)

    PS = [nc.alloc_psum_tensor(f"ps{i}", [128, 512], F32) for i in range(4)]
    PTBF = nc.alloc_psum_tensor("ptbf", [128, 2048], BF16)
    PAUX = nc.alloc_psum_tensor("paux", [128, 512], F32)
    PT2 = nc.alloc_psum_tensor("pt2", [128, 512], F32)

    xr = [ctx.sbuf(f"xr{j}", [128, D], F32) for j in range(TPP)]
    xn_fm = ctx.sbuf("xn_fm", [128, TPP * D], BF16)
    h_buf = ctx.sbuf("h_buf", [128, NHC * TS], BF16)
    HB = ctx.offs["h_buf"]
    q_fm = ctx.sbuf("q_fm", [128, HEADS * TS], BF16, at=HB)
    attn_fm = ctx.sbuf("attn_fm", [128, HEADS * TS], BF16, at=HB + HEADS * TS * 2)
    dummy = ctx.sbuf("dummy", [128, D], F32, at=HB + 2 * HEADS * TS * 2)

    k_fm = [ctx.sbuf(f"k_fm{s}", [128, S], BF16) for s in range(2)]
    v_tm = [ctx.sbuf(f"v_tm{s}", [128, S], BF16) for s in range(2)]
    U = ctx.sbuf("U", [128, 2 * S], BF16)
    U2 = ctx.sbuf("U2", [128, 2 * S], BF16)
    UT = ctx.sbuf("UT", [128, 1024], BF16)
    xntm = ctx.sbuf("xntm", [128, 2 * D], BF16)
    cosb = ctx.sbuf("cosb", [128, TW], F32)
    sinab = ctx.sbuf("sinab", [128, TW], F32)
    masknegb = ctx.sbuf("masknegb", [128, 512], BF16)
    idb = ctx.sbuf("identb", [128, 128], BF16)
    idf = ctx.sbuf("identf32", [128, 128], F32)
    ones_bf = ctx.sbuf("ones_bf", [128, 16], BF16)
    wkv_sb = ctx.sbuf("wkv_sb", [128, NK * 256], BF16)
    wqo_buf = ctx.sbuf("wqo_buf", [128, 2 * D], BF16)
    wg_buf = ctx.sbuf("wg_buf", [128, 2 * D], BF16)
    wu_buf = ctx.sbuf("wu_buf", [128, 2 * D], BF16)
    wp_buf = ctx.sbuf("wp_buf", [128, HF // 2], BF16)
    stg = ctx.sbuf("stg", [128, 512], F32)
    stg2 = ctx.sbuf("stg2", [128, 1024], F32)
    hsq = ctx.sbuf("hsq", [128, 1024], BF16)
    sup = ctx.sbuf("sup", [128, 16], F32)
    su = ctx.sbuf("su", [128, 2], F32)
    ru = ctx.sbuf("ru", [128, 2], F32)
    ssb = ctx.sbuf("ssb", [128, TPP], F32)
    inv = ctx.sbuf("inv", [128, TPP], F32)
    stats = ctx.sbuf("stats", [128, 8 * TPP], F32)
    vstg = ctx.sbuf("vstg", [128, 512], BF16)
    cbank = ctx.sbuf("cbank", [128, 8], F32)

    pa = ctx.offs["xr0"]
    p_wf = [ctx.sbuf(f"p_wf{i}", [128, D], F32, at=pa + i * 8192) for i in range(2)]
    p_wb = [ctx.sbuf(f"p_wb{i}", [128, D], BF16, at=pa + 16384 + i * 4096) for i in range(2)]
    p_sg = [ctx.sbuf(f"p_sg{i}", [128, D], BF16, at=pa + 24576 + i * 4096) for i in range(2)]
    p_tq = [ctx.sbuf(f"p_tq{i}", [128, D], BF16, at=HB + i * 4096) for i in range(2)]
    p_zq = [ctx.sbuf(f"p_zq{i}", [128, D], BF16, at=HB + 8192 + i * 4096) for i in range(2)]
    p_wh = [ctx.sbuf(f"p_wh{i}", [128, D], BF16, at=HB + 16384 + i * 4096) for i in range(2)]
    p_ts = [ctx.sbuf(f"p_ts{i}", [128, D], BF16, at=HB + 24576 + i * 4096) for i in range(2)]
    p_gs = [ctx.sbuf(f"p_gs{i}", [128, 32], F32, at=HB + 32768 + i * 160) for i in range(2)]
    p_sh = [ctx.sbuf(f"p_sh{i}", [128, 32], BF16, at=HB + 33120 + i * 64) for i in range(2)]
    p_sf = [ctx.sbuf(f"p_sf{i}", [128, 32], BF16, at=HB + 33248 + i * 64) for i in range(2)]
    p_kvf = ctx.sbuf("p_kvf", [128, NK * 256], F32, at=HB + 33376 + 32)
    p_se = [ctx.sbuf(f"p_se{i}", [128, D], BF16, at=HB + 50176 + i * 4096) for i in range(2)]
    p_zz = [ctx.sbuf(f"p_zz{i}", [128, D], BF16, at=HB + 33408 + i * 4096) for i in range(2)]

    p_wf.append(ctx.sbuf("p_wf2", [128, D], F32, at=ctx.offs["wqo_buf"]))
    p_wb.append(ctx.sbuf("p_wb2", [128, D], BF16, at=ctx.offs["wg_buf"]))
    p_sg.append(ctx.sbuf("p_sg2", [128, D], BF16, at=ctx.offs["wg_buf"] + 4096))
    p_tq.append(ctx.sbuf("p_tq2", [128, D], BF16, at=ctx.offs["wu_buf"]))
    p_zq.append(ctx.sbuf("p_zq2", [128, D], BF16, at=ctx.offs["wu_buf"] + 4096))
    p_zz.append(ctx.sbuf("p_zz2", [128, D], BF16, at=ctx.offs["wp_buf"]))
    p_wh.append(ctx.sbuf("p_wh2", [128, D], BF16, at=ctx.offs["wp_buf"] + 4096))
    p_se.append(ctx.sbuf("p_se2", [128, D], BF16, at=ctx.offs["xn_fm"]))
    p_ts.append(ctx.sbuf("p_ts2", [128, D], BF16, at=ctx.offs["xn_fm"] + 4096))
    p_gs.append(ctx.sbuf("p_gs2", [128, 32], F32, at=ctx.offs["xn_fm"] + 8192))
    p_sf.append(ctx.sbuf("p_sf2", [128, 32], BF16, at=ctx.offs["xn_fm"] + 8352))

    sIN = ctx.sem("sIN")
    sIN2 = ctx.sem("sIN2")
    sXs = [ctx.sem(f"sX{i}") for i in range(4)]
    sWRs = [ctx.sem("sWR0"), ctx.sem("sWR1"), ctx.sem("sWR2")]
    sPs = [ctx.sem("sP0"), ctx.sem("sP1"), ctx.sem("sP2")]
    sWs = [ctx.sem("sW0"), ctx.sem("sW1")]
    sWp = [ctx.sem("sWp0"), ctx.sem("sWp1")]
    sPE = ctx.sem("sPE")
    sSC = ctx.sem("sSC")
    sDV = ctx.sem("sDV")
    sPL = ctx.sem("sPL")
    sOUT = ctx.sem("sOUT")

    def _finish_early0():
        ctx.barrier()
        ctx.dma("scalar", y_io[0, 0:128, :], xr[0][:], sOUT)
        ctx.wait("sync", sOUT)
        ctx.barrier()

    ctx.dma("scalar", masknegb[:], mnegb_io, sIN2)
    ctx.dma("scalar", idb[:], idb_io, sIN2)
    ctx.dma("scalar", idf[:], idf_io, sIN2)
    ctx.dma("scalar", p_kvf[:].rearrange("p (k f) -> p k f", f=256),
            wkv_io.rearrange("(k p) f -> p k f", p=128), sIN2)
    ctx.wait("vector", sIN2)
    ctx.op("vector", lambda e: e.memset(ones_bf[:], 1.0))
    ctx.op("vector", lambda e: e.memset(cbank[:, 0:1], LN_EPS))
    ctx.op("vector", lambda e: e.memset(cbank[:, 1:2], SCALE))
    ctx.op("vector", lambda e: e.memset(cbank[:, 2:3], -EXP_C))
    ctx.op("vector", lambda e: e.memset(cbank[:, 3:4], RMS_EPS))
    ctx.op("vector", lambda e: e.memset(cbank[:, 4:5], 1.0 / HF))
    wkv_mark = ctx.op("vector", lambda e: e.tensor_copy(wkv_sb[:], p_kvf[:]), sem=sDV, fence=True)

    cast_done, write_done = {}, {}
    if stop_stage == "P0":
        _finish_early0()
        return nc

    def prep_attn(w_io, sc, tbase):
        for k in range(NK):
            t = tbase + k
            slot = t % 2
            if t >= 2:
                ctx.wait("sync", sSC, cast_done[t - 2])
            ctx.dma("sync", p_wf[slot][:], w_io[k * 128:(k + 1) * 128, :], sPs[slot])
            ctx.wait("scalar", sPs[slot])
            if t >= 2:
                ctx.wait("scalar", sWRs[slot])
            cast_done[t] = ctx.op(
                "scalar", lambda e, s=slot: e.copy(p_wb[s][:], p_wf[s][:]), sem=sSC)
            ctx.wait("scalar", sSC, cast_done[t])
            for hh in range(2):
                dst = sc[hh * 8:(hh + 1) * 8, :, k * 128:(k + 1) * 128].rearrange("m p f -> p m f")
                srcv = p_wb[slot][:, hh * 1024:(hh + 1) * 1024].rearrange("p (m f) -> p m f", f=128)
                write_done[t] = ctx.dma("scalar", dst, srcv, sWRs[slot])

    prep_attn(wq_io, sc_wq, 0)
    if stop_stage == "P1a":
        _finish_early0()
        return nc
    prep_attn(wo_io, sc_wo, NK)
    ctx.barrier()
    if stop_stage == "P1":
        _finish_early0()
        return nc
    ctx.wait("scalar", sWRs[0])
    ctx.wait("scalar", sWRs[1])
    ctx.wait("sync", sSC)
    ctx.barrier()

    NT_G = 2 * NHC
    NT_P = NK * (HF // D)

    n_ffn = NT_G + NT_P if ffn_tiles is None else ffn_tiles
    m_cast, m_abs, m_red, m_stt1, m_stt2, m_stt3 = {}, {}, {}, {}, {}, {}
    m_T, m_ev, m_tg = {}, {}, {}

    def prep_src(t):
        if t < NT_G:
            return wgu_io[t * 128:(t + 1) * 128, :]
        i = t - NT_G
        kd, hs = i // 4, i % 4
        return wp_io[kd * 128:(kd + 1) * 128, hs * D:(hs + 1) * D]

    def prep_front(t):
        slot = t % 3
        if t >= 3:
            ctx.wait("sync", sSC, m_cast[t - 3])
        ctx.dma("sync", p_wf[slot][:], prep_src(t), sPs[slot])
        ctx.wait("scalar", sPs[slot])
        if t >= 3:
            ctx.wait("scalar", sDV, m_stt3[t - 3])
        m_cast[t] = ctx.op("scalar", lambda e, s=slot: e.copy(p_wb[s][:], p_wf[s][:]),
                           sem=sSC, fence=True)
        ctx.op("scalar", lambda e, s=slot: e.sign(p_sg[s][:], p_wb[s][:]), sem=sSC, fence=True)
        m_abs[t] = ctx.op("scalar", lambda e, s=slot: e.activation(p_tq[s][:], p_wb[s][:], AF.Abs),
                          sem=sSC)
        ctx.wait("vector", sSC, m_cast[t])
        m_red[t] = ctx.op("vector", lambda e, s=slot: e.reduce_sum(
            p_gs[s][:], p_wb[s][:].rearrange("p (g e) -> p g e", e=64), axis=AX.X,
            apply_absolute_value=True), sem=sDV, fence=True)
        m_sf = ctx.op("vector", lambda e, s=slot: e.tensor_scalar(
            p_sf[s][:], p_gs[s][:], 1.0 / 64.0, 1.0e-8, ALU.mult, ALU.max), sem=sDV, fence=True)

    def se_bc(slot):
        return p_sf[slot][:].unsqueeze(-1).broadcast_to([128, 32, 64])

    def prep_quant(t):
        slot = t % 3
        if t <= 1:
            ctx.wait("vector", sDV, wkv_mark)
        ctx.wait("vector", sSC, m_abs[t])
        m_stt1[t] = ctx.op("vector", lambda e, s=slot: e.scalar_tensor_tensor(
            p_zq[s][:].rearrange("p (g e) -> p g e", e=64),
            p_tq[s][:].rearrange("p (g e) -> p g e", e=64), 2.0, se_bc(s),
            ALU.mult, ALU.subtract), sem=sDV, fence=True)
        m_stt2[t] = ctx.op("vector", lambda e, s=slot: e.scalar_tensor_tensor(
            p_zz[s][:].rearrange("p (g e) -> p g e", e=64),
            p_zq[s][:].rearrange("p (g e) -> p g e", e=64), BIGK, se_bc(s),
            ALU.mult, ALU.min), sem=sDV, fence=True)
        if t >= 3:
            ctx.wait("vector", sPE, m_T[t - 3])
        m_stt3[t] = ctx.op("vector", lambda e, s=slot: e.scalar_tensor_tensor(
            p_wh[s][:], p_zz[s][:], 0.0, p_sg[s][:], ALU.max, ALU.mult), sem=sDV)

    def prep_tail(t):
        slot = t % 3
        for g in range(2):
            ctx.wait("scalar", sPE, m_tg[t][g])
            if t >= 3 and g == 0:
                ctx.wait("scalar", sWRs[slot])
            m_ev[t] = ctx.op("scalar", lambda e, s=slot, g=g: e.copy(
                p_ts[s][:, g * 1024:(g + 1) * 1024],
                PTBF[:, g * 1024:(g + 1) * 1024]), sem=sSC)
        if t < NT_G:
            dstw = sc_wgu[t]
        else:
            i = t - NT_G
            kd, hs = i // 4, i % 4
            dstw = sc_wp[kd, :, hs * D:(hs + 1) * D]
        ctx.wait("scalar", sSC, m_ev[t])
        ctx.dma("scalar", dstw, p_ts[slot][:], sWRs[slot])

    def prep_T(t):
        slot = t % 3
        ctx.wait("tensor", sDV, m_stt3[t])
        m_tg[t] = {}
        for g in range(2):
            if t >= 1:
                ctx.wait("tensor", sSC, m_ev.get(t - 1, 0))
            gm = None
            for kk in range(8):
                k = g * 8 + kk
                gm = ctx.op("tensor", lambda e, s=slot, g=g, k=k, kk=kk: e.transpose(
                    PTBF[:, g * 1024 + kk * 128:g * 1024 + (kk + 1) * 128],
                    p_wh[s][:, k * 128:(k + 1) * 128], idb[:]),
                    sem=sPE if kk == 7 else None)
            m_tg[t][g] = gm
        m_T[t] = m_tg[t][1]

    for t in range(n_ffn + 1):
        if t >= 1:
            prep_quant(t - 1)
        if t < n_ffn:
            prep_front(t)
        if t >= 2:
            prep_tail(t - 2)
        if t >= 1:
            prep_T(t - 1)
    if n_ffn >= 1:
        prep_tail(n_ffn - 1)


    if ffn_tiles is not None:
        _finish_early0()
        return nc
    ctx.barrier()
    ctx.wait("sync", sWRs[0])
    ctx.wait("sync", sWRs[1])
    ctx.wait("sync", sWRs[2])
    ctx.wait("sync", sIN)
    ctx.wait("sync", sIN2)
    ctx.barrier()

    def _finish_early():
        ctx.dma("scalar", y_io[0, 0:128, :], xr[0][:], sOUT)
        ctx.wait("sync", sOUT)
        ctx.barrier()

    if stop_stage == "P":
        _finish_early()
        return nc

    def fm_rhs(k):
        v = xn_fm[:].rearrange("p (j k f) -> p k j f", k=NK, f=128)
        return v[:, k]

    ptbf_g = {0: None, 1: None}

    def wait_ptbf(eng, half):
        m = ptbf_g[half]
        if m is not None:
            ctx.wait(eng, m[0], m[1])

    def ln_block(src_tiles):
        stm = {}
        for j in range(TPP):
            xt = src_tiles[j]
            st = stats[:, j * 8:(j + 1) * 8]
            ctx.wait("scalar", sXs[j])
            ctx.op("scalar", lambda e, x=xt, st=st: e.activation(
                dummy[:], x[:], AF.Identity, accum_out=st[:, 0:1]), sem=sSC, fence=True)
            stm[j] = ctx.op("scalar", lambda e, x=xt, st=st: e.activation(
                dummy[:], x[:], AF.Square, accum_out=st[:, 1:2]), sem=sSC, fence=True)
        tpd = {}
        ev_half = {0: 0, 1: 0}
        for j in range(TPP):
            xt = src_tiles[j]
            st = stats[:, j * 8:(j + 1) * 8]
            xsl = j % 2
            ctx.wait("vector", sSC, stm[j])
            ctx.op("vector", lambda e, st=st: e.tensor_scalar(
                st[:, 2:3], st[:, 0:1], 1.0 / D, None, ALU.mult), fence=True)
            ctx.op("vector", lambda e, st=st: e.tensor_scalar(
                st[:, 3:4], st[:, 1:2], 1.0 / D, None, ALU.mult), fence=True)
            ctx.op("vector", lambda e, st=st: e.tensor_tensor(
                st[:, 4:5], st[:, 2:3], st[:, 2:3], ALU.mult), fence=True)
            v1 = ctx.op("vector", lambda e, st=st: e.tensor_sub(
                st[:, 5:6], st[:, 3:4], st[:, 4:5]), sem=sDV, fence=True)
            ctx.wait("scalar", sDV, v1)
            s1 = ctx.op("scalar", lambda e, st=st: e.activation(
                st[:, 6:7], st[:, 5:6], AF.Sqrt, bias=cbank[:, 0:1]), sem=sSC)
            ctx.wait("vector", sSC, s1)
            ctx.op("vector", lambda e, st=st: e.reciprocal(st[:, 7:8], st[:, 6:7]), fence=True)
            v2 = ctx.op("vector", lambda e, st=st: e.scalar_tensor_tensor(
                st[:, 4:5], st[:, 2:3], -1.0, st[:, 7:8], ALU.mult, ALU.mult), sem=sDV,
                fence=True)
            ctx.wait("scalar", sDV, v2)
            if j >= 2:
                ctx.wait("scalar", sPE, tpd[j - 2])
            s2 = ctx.op("scalar", lambda e, x=xt, st=st, xs=xsl: e.activation(
                xntm[:, xs * D:(xs + 1) * D], x[:], AF.Identity,
                scale=st[:, 7:8], bias=st[:, 4:5]), sem=sSC)
            ctx.wait("tensor", sSC, s2)
            for g in range(2):
                wait_ptbf("tensor", g)
                for kk in range(8):
                    k = g * 8 + kk
                    tpd[j] = ctx.op("tensor", lambda e, g=g, k=k, kk=kk, xs=xsl: e.transpose(
                        PTBF[:, g * 1024 + kk * 128:g * 1024 + (kk + 1) * 128],
                        xntm[:, xs * D + k * 128:xs * D + (k + 1) * 128], idb[:]),
                        sem=sPE if kk == 7 else None)
                ctx.wait("vector", sPE, tpd[j])
                ev_half[g] = ctx.op("vector", lambda e, g=g, j=j: e.tensor_copy(
                    xn_fm[:, j * D + g * 1024:j * D + (g + 1) * 1024],
                    PTBF[:, g * 1024:(g + 1) * 1024]), sem=sDV)
                ptbf_g[g] = (sDV, ev_half[g])
        return max(ev_half[0], ev_half[1])

    out_count = {}
    for p in range(NPASS):
        tlist = [(s, p * TPS + jj) for s in range(2) for jj in range(TPS)]

        if p >= 1:
            ctx.wait("sync", sOUT, out_count[p - 1])
        for j, (s, ti) in enumerate(tlist):
            ctx.dma("sync", xr[j][:], x_io[s, ti * 128:(ti + 1) * 128, :], sXs[j])
        c0 = p * TW
        ctx.dma("scalar", cosb[:], cos_io[:, c0:c0 + TW], sIN2)
        ctx.dma("scalar", sinab[:], sina_io[:, c0:c0 + TW], sIN2)
        lnm1 = ln_block(xr)

        if stop_stage == "A":
            _finish_early()
            return nc
        ctx.wait("tensor", sDV, lnm1)
        for k in range(NK):
            ctx.op("tensor", lambda e, k=k, r=fm_rhs(k): e.matmul(
                PS[0][:], wkv_sb[:, k * 256:k * 256 + 128], r,
                start=(k == 0), stop=(k == NK - 1)),
                sem=sPE if k == NK - 1 else None)
        k_pe = ctx.val(sPE)
        for k in range(NK):
            ctx.op("tensor", lambda e, k=k, r=fm_rhs(k): e.matmul(
                PS[1][:], wkv_sb[:, k * 256 + 128:k * 256 + 256], r,
                start=(k == 0), stop=(k == NK - 1)),
                sem=sPE if k == NK - 1 else None)
        v_pe = ctx.val(sPE)
        ctx.wait("vector", sIN2)
        ctx.wait("vector", sPE, k_pe)
        for s in range(2):
            b = s * TW
            dstk = k_fm[s][:, c0:c0 + TW]
            ctx.op("vector", lambda e, b=b: e.tensor_mul(
                stg[:, 0:TW], PS[0][:, b:b + TW], cosb[:]), fence=True)
            ctx.op("vector", lambda e, b=b: e.tensor_mul(
                stg2[0:64, 0:TW], PS[0][64:128, b:b + TW], sinab[0:64, :]), fence=True)
            ctx.op("vector", lambda e, b=b: e.tensor_mul(
                stg2[64:128, 0:TW], PS[0][0:64, b:b + TW], sinab[64:128, :]), fence=True)
            krope_mark = ctx.op("vector", lambda e, d=dstk: e.tensor_add(
                d, stg[:, 0:TW], stg2[:, 0:TW]), sem=sDV, fence=True)
        ctx.wait("scalar", sPE, v_pe)
        vc = ctx.op("scalar", lambda e: e.copy(vstg[:, 0:TS], PS[1][:, 0:TS]), sem=sSC)
        ctx.wait("tensor", sSC, vc)
        vtd = {}
        for j, (s, ti) in enumerate(tlist):
            half = j % 2
            wait_ptbf("tensor", half)
            tmark = ctx.op("tensor", lambda e, h=half, j=j: e.transpose(
                PTBF[:, h * 1024:h * 1024 + 128], vstg[:, j * 128:(j + 1) * 128], idb[:]), sem=sPE)
            ctx.wait("vector", sPE, tmark)
            vtd[j] = ctx.op("vector", lambda e, h=half, s=s, ti=ti: e.tensor_copy(
                v_tm[s][:, ti * 128:(ti + 1) * 128], PTBF[:, h * 1024:h * 1024 + 128]), sem=sDV)
            ptbf_g[half] = (sDV, vtd[j])
        b_krope = krope_mark

        if stop_stage == "B":
            _finish_early()
            return nc
        ctx.dma("sync", wqo_buf[:, 0:D], sc_wq[0], sWs[0])
        ctx.dma("sync", wqo_buf[:, D:2 * D], sc_wq[1], sWs[1])
        rope_done = {}
        for mi in range(HEADS):
            slot = mi % 2
            ctx.wait("tensor", sWs[slot])
            if mi == 0:
                ctx.wait("tensor", sDV, b_krope)
            if mi == 1:
                ctx.wait("tensor", sSC, vc)
            if mi >= 2:
                ctx.wait("tensor", sDV, rope_done[mi - 2])
            mmv = None
            for k in range(NK):
                mmv = ctx.op("tensor", lambda e, sl=slot, k=k, r=fm_rhs(k): e.matmul(
                    PS[sl][:], wqo_buf[:, sl * D + k * 128:sl * D + (k + 1) * 128], r,
                    start=(k == 0), stop=(k == NK - 1)),
                    sem=sPE if k == NK - 1 else None)
            if mi + 2 < HEADS:
                ctx.wait("sync", sPE, mmv)
                ctx.dma("sync", wqo_buf[:, slot * D:(slot + 1) * D], sc_wq[mi + 2], sWs[slot])
            ctx.wait("vector", sPE, mmv)
            for s in range(2):
                b = s * TW
                ctx.op("vector", lambda e, sl=slot, b=b: e.tensor_mul(
                    stg[:, 0:TW], PS[sl][:, b:b + TW], cosb[:]), fence=True)
                ctx.op("vector", lambda e, sl=slot, b=b: e.tensor_mul(
                    stg2[0:64, 0:TW], PS[sl][64:128, b:b + TW], sinab[0:64, :]), fence=True)
                ctx.op("vector", lambda e, sl=slot, b=b: e.tensor_mul(
                    stg2[64:128, 0:TW], PS[sl][0:64, b:b + TW], sinab[64:128, :]), fence=True)
                rope_done[mi] = ctx.op("vector", lambda e, mi=mi, b=b: e.tensor_add(
                    q_fm[:, mi * TS + b:mi * TS + b + TW], stg[:, 0:TW], stg2[:, 0:TW]),
                    sem=sDV, fence=True)
        c_mm_last = ctx.val(sPE)

        if stop_stage == "C":
            _finish_early()
            return nc
        NU = len(tlist) * HEADS
        exp_last, un_mark, t_mark, av_mark, aev_mark = {}, {}, {}, {}, {}
        ut_av = {0: 0, 1: 0}
        ps_exp = {0: (sDV, rope_done[HEADS - 2]), 1: (sDV, rope_done[HEADS - 1]),
                  2: None, 3: None}
        gctr = [0]
        sctr = [2]

        def d_front(u):
            j, h = u // HEADS, u % HEADS
            s, ti = tlist[j]
            nk = ti + 1
            ngrp = (nk + 3) // 4
            par = u % 2
            qh = q_fm[:, h * TS + j * 128:h * TS + (j + 1) * 128]
            ctx.wait("tensor", sDV, rope_done[h])
            for g in range(ngrp):
                w = min(4, nk - 4 * g)
                bank = sctr[0] % 4
                sctr[0] += 1
                has_diag = (4 * g <= nk - 1 < 4 * g + w)
                if ps_exp[bank] is not None:
                    ctx.wait("tensor", ps_exp[bank][0], ps_exp[bank][1])
                mm = ctx.op("tensor", lambda e, b=bank, g=g, w=w, qh=qh, s=s: e.matmul(
                    PS[b][:, 0:w * 128], qh, k_fm[s][:, g * 512:g * 512 + w * 128],
                    start=True, stop=not has_diag),
                    sem=None if has_diag else sPE)
                if has_diag:
                    mm = ctx.op("tensor", lambda e, b=bank, w=w: e.matmul(
                        PS[b][:, 0:w * 128], idb[:], masknegb[:, (4 - w) * 128:512],
                        start=False, stop=True), sem=sPE)
                ctx.wait("scalar", sPE, mm)
                if g == 0 and u >= 2:
                    ctx.wait("scalar", sDV, un_mark[u - 2])
                exp_last[u] = ctx.op("scalar", lambda e, b=bank, g=g, w=w, par=par: e.activation(
                    U[:, par * S + g * 512:par * S + g * 512 + w * 128],
                    PS[b][:, 0:w * 128], AF.Exp,
                    bias=cbank[:, 2:3], scale=cbank[:, 1:2],
                    accum_out=sup[:, par * 8 + g:par * 8 + g + 1]), sem=sSC)
                ps_exp[bank] = (sSC, exp_last[u])

        def d_tail(u):
            j, h = u // HEADS, u % HEADS
            s, ti = tlist[j]
            nk = ti + 1
            ngrp = (nk + 3) // 4
            par = u % 2
            ctx.wait("vector", sSC, exp_last[u])
            ctx.op("vector", lambda e, par=par, n=ngrp: e.reduce_sum(
                su[:, par:par + 1], sup[:, par * 8:par * 8 + n], axis=AX.X), fence=True)
            ctx.op("vector", lambda e, par=par: e.reciprocal(
                ru[:, par:par + 1], su[:, par:par + 1]), fence=True)
            if u >= 2:
                ctx.wait("vector", sPE, t_mark[u - 2])
            un_mark[u] = ctx.op("vector", lambda e, par=par, nk=nk: e.tensor_scalar(
                U2[:, par * S:par * S + nk * 128], U[:, par * S:par * S + nk * 128],
                ru[:, par:par + 1], None, ALU.mult), sem=sDV)
            ctx.wait("tensor", sDV, un_mark[u])
            if u >= 2:
                ctx.wait("tensor", sSC, aev_mark[u - 2])
            for g in range(ngrp):
                w = min(4, nk - 4 * g)
                slot = gctr[0] % 2
                wait_ptbf("tensor", slot)
                tm = None
                for kk in range(w):
                    kt = g * 4 + kk
                    tm = ctx.op("tensor", lambda e, sl=slot, kt=kt, kk=kk, par=par: e.transpose(
                        PTBF[:, sl * 1024 + kk * 128:sl * 1024 + (kk + 1) * 128],
                        U2[:, par * S + kt * 128:par * S + (kt + 1) * 128], idb[:]),
                        sem=sPE if kk == w - 1 else None)
                t_mark[u] = tm
                eeng = "scalar" if gctr[0] % 2 == 0 else "vector"
                esem = sSC if gctr[0] % 2 == 0 else sDV
                ctx.wait(eeng, sPE, tm)
                ctx.wait(eeng, sPE, ut_av[slot])
                ev = ctx.op(eeng, lambda e, sl=slot, w=w: (
                    e.copy if hasattr(e, "copy") else e.tensor_copy)(
                    UT[:, sl * 512:sl * 512 + w * 128],
                    PTBF[:, sl * 1024:sl * 1024 + w * 128]), sem=esem)
                ptbf_g[slot] = (esem, ev)
                ctx.wait("tensor", esem, ev)
                for kk in range(w):
                    kt = g * 4 + kk
                    av_mark[u] = ctx.op("tensor", lambda e, sl=slot, kk=kk, kt=kt, nk=nk, s=s, par=par: e.matmul(
                        PAUX[:, par * 128:(par + 1) * 128],
                        v_tm[s][:, kt * 128:(kt + 1) * 128],
                        UT[:, sl * 512 + kk * 128:sl * 512 + (kk + 1) * 128],
                        start=(kt == 0), stop=(kt == nk - 1), skip_group_check=True),
                        sem=sPE if kk == w - 1 else None)
                ut_av[slot] = av_mark[u]
                gctr[0] += 1
            ctx.wait("scalar", sPE, av_mark[u])
            aev_mark[u] = ctx.op("scalar", lambda e, h=h, j=j, par=par: e.copy(
                attn_fm[:, h * TS + j * 128:h * TS + (j + 1) * 128],
                PAUX[:, par * 128:(par + 1) * 128]), sem=sSC)

        for u in range(NU + 1):
            if u < NU:
                d_front(u)
            if u >= 1:
                d_tail(u - 1)
        d_ps_exp = ps_exp
        aev_last = aev_mark[NU - 1]

        if stop_stage == "D":
            _finish_early()
            return nc
        ctx.wait("sync", sPE, c_mm_last)
        ctx.dma("sync", wqo_buf[:, 0:D], sc_wo[0], sWs[0])
        ctx.dma("sync", wqo_buf[:, D:2 * D], sc_wo[1], sWs[1])
        add_done, ev_done, tp_done = {}, {}, {}
        for mo in range(NK + 1):
            slot = mo % 2
            mmv = None
            if mo < NK:
                ctx.wait("tensor", sWs[slot])
                if mo == 0:
                    ctx.wait("tensor", sSC, aev_last)
                if mo < 2:
                    m = d_ps_exp[mo]
                    if m is not None:
                        ctx.wait("tensor", m[0], m[1])
                if mo >= 2:
                    ctx.wait("tensor", sSC, ev_done[mo - 2])
                for ki in range(NK):
                    mmv = ctx.op("tensor", lambda e, sl=slot, ki=ki: e.matmul(
                        PS[sl][:], wqo_buf[:, sl * D + ki * 128:sl * D + (ki + 1) * 128],
                        attn_fm[:, ki * TS:(ki + 1) * TS],
                        start=(ki == 0), stop=(ki == NK - 1)),
                        sem=sPE if ki == NK - 1 else None)
            if mo >= 1:
                pc = mo - 1
                pslot = pc % 2
                pps = PT2 if pslot == 0 else PAUX
                ctx.wait("tensor", sSC, ev_done[pc])
                if pc >= 2:
                    ctx.wait("tensor", sDV, add_done[pc - 2])
                for j in range(TPP):
                    tp_done[pc] = ctx.op("tensor", lambda e, j=j, ps2=pps, sl2=pslot: e.transpose(
                        ps2[:, j * 128:(j + 1) * 128],
                        stg2[:, sl2 * 512 + j * 128:sl2 * 512 + (j + 1) * 128], idf[:]),
                        sem=sPE if j == TPP - 1 else None)
                ctx.wait("vector", sPE, tp_done[pc])
                for j in range(TPP):
                    add_done[pc] = ctx.op("vector", lambda e, j=j, pc=pc, ps2=pps: e.tensor_add(
                        xr[j][:, pc * 128:(pc + 1) * 128], xr[j][:, pc * 128:(pc + 1) * 128],
                        ps2[:, j * 128:(j + 1) * 128]),
                        sem=sDV if j == TPP - 1 else None)
            if mo >= NK:
                break
            if mo + 2 < NK:
                ctx.wait("sync", sPE, mmv)
                ctx.dma("sync", wqo_buf[:, slot * D:(slot + 1) * D], sc_wo[mo + 2], sWs[slot])
            ctx.wait("scalar", sPE, mmv)
            if mo >= 2:
                ctx.wait("scalar", sDV, add_done[mo - 2])
            ev_done[mo] = ctx.op("scalar", lambda e, sl=slot: e.copy(
                stg2[:, sl * 512:(sl + 1) * 512], PS[sl][:]), sem=sSC)
        e_mm_last = ctx.val(sPE)
        e_ev_done = ev_done
        e_add_last = add_done[NK - 1]

        if stop_stage == "E":
            _finish_early()
            return nc
        ctx.wait("scalar", sDV, e_add_last)
        lnm2 = ln_block(xr)

        if stop_stage == "F":
            _finish_early()
            return nc
        ctx.wait("sync", sPE, e_mm_last)
        for hc in range(2):
            ctx.dma("sync", wg_buf[:, (hc % 2) * D:(hc % 2 + 1) * D], sc_wgu[hc], sWs[hc % 2])
            ctx.dma("sync", wu_buf[:, (hc % 2) * D:(hc % 2 + 1) * D], sc_wgu[NHC + hc], sWs[hc % 2])
        HH = HF // 4
        ctx.dma("sync", wp_buf[:, 0:HH], sc_wp[0, :, 0:HH], sWp[0])
        ctx.dma("sync", wp_buf[:, HH:2 * HH], sc_wp[0, :, HH:2 * HH], sWp[1])
        hv_done, ss_done, h2_done, g_sl1 = {}, {}, {}, {}
        for hc in range(NHC + 1):
            slot = hc % 2
            ps_g, ps_u = PS[2 * slot], PS[2 * slot + 1]
            mmv = None
            if hc < NHC:
                ctx.wait("tensor", sWs[slot])
                if hc == 0:
                    ctx.wait("tensor", sDV, lnm2)
                    ctx.wait("tensor", sSC, e_ev_done[NK - 1])
                if hc == 1:
                    for m in (d_ps_exp[2], d_ps_exp[3]):
                        if m is not None:
                            ctx.wait("tensor", m[0], m[1])
                if hc >= 2:
                    ctx.wait("tensor", sDV, hv_done[hc - 2])
                for k in range(NK):
                    ctx.op("tensor", lambda e, pg=ps_g, sl=slot, k=k, r=fm_rhs(k): e.matmul(
                        pg[:], wg_buf[:, sl * D + k * 128:sl * D + (k + 1) * 128], r,
                        start=(k == 0), stop=(k == NK - 1)))
                for k in range(NK):
                    mmv = ctx.op("tensor", lambda e, pu=ps_u, sl=slot, k=k, r=fm_rhs(k): e.matmul(
                        pu[:], wu_buf[:, sl * D + k * 128:sl * D + (k + 1) * 128], r,
                        start=(k == 0), stop=(k == NK - 1)),
                        sem=sPE if k == NK - 1 else None)
            if hc >= 1:
                pc = hc - 1
                if pc == 0:
                    ctx.wait("tensor", sDV, e_add_last)
                ctx.wait("tensor", sSC, h2_done[pc])
                for j in range(TPP):
                    ss_done[pc] = ctx.op("tensor", lambda e, j=j, pc=pc, hq=pc % 2: e.matmul(
                        PAUX[:, 128 + j:129 + j],
                        hsq[:, hq * 512 + j * 128:hq * 512 + (j + 1) * 128],
                        ones_bf[:, 0:1],
                        start=(pc == 0), stop=(pc == NHC - 1), skip_group_check=True),
                        sem=sPE if j == TPP - 1 else None)
            if hc >= NHC:
                break
            if hc + 2 < NHC:
                ctx.wait("sync", sPE, mmv)
                ctx.dma("sync", wg_buf[:, slot * D:(slot + 1) * D], sc_wgu[hc + 2], sWs[slot])
                ctx.dma("sync", wu_buf[:, slot * D:(slot + 1) * D], sc_wgu[NHC + hc + 2], sWs[slot])
            ctx.wait("scalar", sPE, mmv)
            if hc >= 2:
                ctx.wait("scalar", sDV, hv_done[hc - 2])
            sl1 = ctx.op("scalar", lambda e, pg=ps_g, sl=slot: e.activation(
                stg[:, 0:512] if sl == 0 else stg2[:, 0:512], pg[:], AF.Silu), sem=sSC)
            g_sl1[hc] = sl1
            ctx.wait("vector", sSC, sl1)
            hv_done[hc] = ctx.op("vector", lambda e, pu=ps_u, hc=hc, sl=slot: e.tensor_mul(
                h_buf[:, hc * TS:(hc + 1) * TS],
                stg[:, 0:512] if sl == 0 else stg2[:, 0:512], pu[:]), sem=sDV)
            ctx.wait("scalar", sDV, hv_done[hc])
            if hc >= 2:
                ctx.wait("scalar", sPE, ss_done[hc - 2])
            h2_done[hc] = ctx.op("scalar", lambda e, hc=hc, hq=slot: e.activation(
                hsq[:, hq * 512:(hq + 1) * 512], h_buf[:, hc * TS:(hc + 1) * TS],
                AF.Square), sem=sSC)
        g_ss_last = ss_done[NHC - 1]
        g_hv = hv_done

        if stop_stage == "G":
            _finish_early()
            return nc
        ctx.wait("vector", sPE, g_ss_last)
        hv = ctx.op("vector", lambda e: e.tensor_copy(ssb[:], PAUX[:, 128:128 + TPP]), sem=sDV)
        ctx.wait("scalar", sDV, hv)
        sh = ctx.op("scalar", lambda e: e.activation(
            stats[:, 0:TPP], ssb[:], AF.Sqrt, scale=cbank[:, 4:5], bias=cbank[:, 3:4]), sem=sSC)
        ctx.wait("vector", sSC, sh)
        inv_mark = ctx.op("vector", lambda e: e.reciprocal(inv[:], stats[:, 0:TPP]), sem=sDV)
        h_hv_mark = hv

        if stop_stage == "H":
            _finish_early()
            return nc
        NHH = NHC // 4
        NST = 4
        add_done, ev_done, tp_done = {}, {}, {}
        for mo in range(NK + 1):
            slot = mo % 2
            mmv = None
            if mo < NK:
                if mo >= 2:
                    ctx.wait("tensor", sSC, ev_done[mo - 2])
                if mo == 0:
                    ctx.wait("tensor", sSC, g_sl1[NHC - 2])
                    ctx.wait("tensor", sDV, g_hv[NHC - 2])
                if mo == 1:
                    ctx.wait("tensor", sSC, g_sl1[NHC - 1])
                    ctx.wait("tensor", sDV, g_hv[NHC - 1])
                for h in range(NST):
                    step = NST * mo + h
                    bslot = step % 2
                    ctx.wait("tensor", sWp[bslot])
                    for hk_l in range(NHH):
                        hk = h * NHH + hk_l
                        if mo == 0:
                            ctx.wait("tensor", sDV, g_hv[hk])
                        mmv = ctx.op("tensor", lambda e, sl=slot, bs=bslot, hk=hk, hk_l=hk_l: e.matmul(
                            PS[sl][:], wp_buf[:, bs * HH + hk_l * 128:bs * HH + (hk_l + 1) * 128],
                            h_buf[:, hk * TS:(hk + 1) * TS],
                            start=(hk == 0), stop=(hk == NHC - 1)),
                            sem=sPE if hk_l == NHH - 1 else None)
                    nstep = step + 2
                    if nstep < NST * NK:
                        nmo, nh = nstep // NST, nstep % NST
                        ctx.wait("sync", sPE, mmv)
                        ctx.dma("sync", wp_buf[:, bslot * HH:(bslot + 1) * HH],
                                sc_wp[nmo, :, nh * HH:(nh + 1) * HH], sWp[bslot])
            if mo >= 1:
                pc = mo - 1
                pslot = pc % 2
                pps = PT2 if pslot == 0 else PAUX
                ctx.wait("tensor", sSC, ev_done[pc])
                if pc == 0:
                    ctx.wait("tensor", sDV, e_add_last)
                if pc == 1:
                    ctx.wait("tensor", sDV, h_hv_mark)
                if pc >= 2:
                    ctx.wait("tensor", sDV, add_done[pc - 2])
                for j in range(TPP):
                    tp_done[pc] = ctx.op("tensor", lambda e, j=j, ps2=pps, sl2=pslot: e.transpose(
                        ps2[:, j * 128:(j + 1) * 128],
                        stg2[:, sl2 * 512 + j * 128:sl2 * 512 + (j + 1) * 128], idf[:]),
                        sem=sPE if j == TPP - 1 else None)
                ctx.wait("vector", sPE, tp_done[pc])
                if pc == 0:
                    ctx.wait("vector", sDV, inv_mark)
                for j in range(TPP):
                    add_done[pc] = ctx.op("vector", lambda e, j=j, pc=pc, ps2=pps: e.scalar_tensor_tensor(
                        xr[j][:, pc * 128:(pc + 1) * 128],
                        ps2[:, j * 128:(j + 1) * 128], inv[:, j:j + 1],
                        xr[j][:, pc * 128:(pc + 1) * 128],
                        ALU.mult, ALU.add),
                        sem=sDV if j == TPP - 1 else None)
            if mo >= NK:
                break
            ctx.wait("scalar", sPE, mmv)
            if mo >= 2:
                ctx.wait("scalar", sDV, add_done[mo - 2])
            ev_done[mo] = ctx.op("scalar", lambda e, sl=slot: e.copy(
                stg2[:, sl * 512:(sl + 1) * 512], PS[sl][:]), sem=sSC)
        i_add_last = add_done[NK - 1]

        if stop_stage == "I":
            _finish_early()
            return nc
        ctx.wait("scalar", sDV, i_add_last)
        for j, (s, ti) in enumerate(tlist):
            ctx.dma("scalar", y_io[s, ti * 128:(ti + 1) * 128, :], xr[j][:], sOUT)
        out_count[p] = ctx.val(sOUT)
        ctx.barrier()

    ctx.wait("sync", sOUT)
    ctx.barrier()
    return nc


_CACHE = {}


def _get_nc(S):
    if S not in _CACHE:
        _CACHE[S] = build_nc(S)
    return _CACHE[S]


def _tables(S):
    inv_freq = 1.0 / (10000.0 ** (np.arange(0, DH, 2, dtype=np.float64) / DH))
    pos = np.arange(S, dtype=np.float64)[:, None] * inv_freq[None, :]
    cos = np.cos(pos)
    sin = np.sin(pos)
    cos_t = np.concatenate([cos, cos], axis=1).T.astype(np.float32)
    sina_t = np.concatenate([-sin, sin], axis=1).T.astype(np.float32)
    return np.ascontiguousarray(cos_t), np.ascontiguousarray(sina_t)


def kernel(L0_attn_x, L0_attn_Wq, L0_attn_Wkv, L0_attn_Wout, L0_ffn_w_gate_up, L0_ffn_w_proj):
    x = np.ascontiguousarray(np.asarray(L0_attn_x, dtype=np.float32))
    wq = np.ascontiguousarray(np.asarray(L0_attn_Wq, dtype=np.float32))
    wkv = np.ascontiguousarray(np.asarray(L0_attn_Wkv, dtype=np.float32))
    wo = np.ascontiguousarray(np.asarray(L0_attn_Wout, dtype=np.float32))
    wgu = np.ascontiguousarray(np.asarray(L0_ffn_w_gate_up, dtype=np.float32))
    wp = np.ascontiguousarray(np.asarray(L0_ffn_w_proj, dtype=np.float32))
    B, S, _ = x.shape
    n_cores = 8
    per = B // n_cores
    cos_t, sina_t = _tables(S)
    mneg_np = np.triu(np.full((128, 128), MASK_NEG, dtype=np.float32), 1)
    mnegb_np = np.zeros((128, 512), dtype=BFNP)
    mnegb_np[:, 384:512] = mneg_np.astype(BFNP)
    idb_np = np.eye(128, dtype=BFNP)
    idf_np = np.eye(128, dtype=np.float32)
    nc = _get_nc(S)
    in_maps = []
    for c in range(n_cores):
        in_maps.append({
            "x": np.ascontiguousarray(x[c * per:(c + 1) * per]),
            "wq": wq, "wkv": wkv, "wo": wo, "wgu": wgu, "wp": wp,
            "costab": cos_t, "sinatab": sina_t,
            "maskneg": mneg_np, "masknegbf": mnegb_np, "identbf": idb_np, "identf": idf_np,
        })
    res = run_bass_kernel_spmd(nc, in_maps, core_ids=list(range(n_cores)))
    out = np.concatenate([res.results[c]["y"] for c in range(n_cores)], axis=0)
    return np.ascontiguousarray(out.astype(np.float32))


# version 18 (final)
# speedup vs baseline: 1.5103x; 1.0114x over previous
import numpy as np
import ml_dtypes
import concourse.bass as bass
import concourse.mybir as mybir
from concourse.bass_utils import run_bass_kernel_spmd

F32 = mybir.dt.float32
BF16 = mybir.dt.bfloat16
AF = mybir.ActivationFunctionType
ALU = mybir.AluOpType
AX = mybir.AxisListType
BFNP = ml_dtypes.bfloat16

D = 2048
HEADS = 16
DH = 128
HF = 8192
NK = D // 128
NHC = HF // 128
SCALE = DH ** -0.5
EXP_C = 4.0
MASK_NEG = -1.0e9
LN_EPS = 1e-5
RMS_EPS = float(np.finfo(np.float32).eps)
BIGK = 1.0e12
ENGS = ("sync", "scalar", "vector", "tensor", "gpsimd")


class Ctx:
    def __init__(self, nc):
        self.nc = nc
        self.streams = {k: [] for k in ENGS}
        self.sems = {}
        self.offs = {}
        base = nc.sbuf_base
        self.sb_off = (base + 31) & ~31
        self.sb_limit = base + nc.sbuf_bytes_remaining

    def sbuf(self, name, shape, dtype, at=None):
        nbytes = int(np.prod(shape[1:])) * (2 if dtype == BF16 else 4)
        if at is None:
            off = self.sb_off
            self.sb_off = (off + nbytes + 31) & ~31
            assert self.sb_off <= self.sb_limit, f"SBUF overflow at {name}"
        else:
            off = at
            assert off % 32 == 0, name
        t = self.nc.alloc_sbuf_tensor_at(name, list(shape), dtype, offset=off)
        self.offs[name] = off
        return t

    def sem(self, name):
        h = self.nc.alloc_semaphore(name)
        self.sems[name] = [h, 0]
        return name

    def val(self, name):
        return self.sems[name][1]

    def emit(self, eng, f):
        self.streams[eng].append(f)

    def wait(self, eng, name, v=None):
        h, cur = self.sems[name]
        v = cur if v is None else v
        if v <= 0:
            return
        self.emit(eng, lambda e, h=h, v=v: e.wait_ge(h, v))

    def op(self, eng, fn, sem=None, by=1, fence=False):
        if fence and sem is None:
            sem = {"scalar": "sSC", "vector": "sDV", "gpsimd": "sPL", "tensor": "sPE"}[eng]
        if sem is not None:
            rec = self.sems[sem]
            rec[1] += by
            h = rec[0]
            self.emit(eng, lambda e, fn=fn, h=h, by=by: fn(e).then_inc(h, by))
            if fence:
                self.emit(eng, lambda e, h=h, v=rec[1]: e.wait_ge(h, v))
            return rec[1]
        self.emit(eng, fn)
        return None

    def dma(self, eng, out, in_, sem):
        return self.op(eng, lambda e, o=out, i=in_: e.dma_start(o, i), sem=sem, by=16)

    def barrier(self):
        nc = self.nc
        streams, self.streams = self.streams, {k: [] for k in ENGS}
        if not any(streams.values()):
            return
        with nc.Block() as blk:
            for k in ENGS:
                fns = streams[k]
                if fns:
                    getattr(blk, k)(lambda e, fns=fns: [f(e) for f in fns])


def build_nc(S, stop_stage=None, ffn_tiles=None):
    TSEQ = S // 128
    assert TSEQ % 2 == 0
    NPASS = TSEQ // 2
    TPS = 2
    TPP = 2 * TPS
    TS = 128 * TPP
    TW = 128 * TPS

    nc = bass.Bass("TRN2", target_bir_lowering=False)

    x_io = nc.dram_tensor("x", [2, S, D], F32, kind="ExternalInput").ap()
    wq_io = nc.dram_tensor("wq", [D, D], F32, kind="ExternalInput").ap()
    wkv_io = nc.dram_tensor("wkv", [D, 2 * DH], F32, kind="ExternalInput").ap()
    wo_io = nc.dram_tensor("wo", [D, D], F32, kind="ExternalInput").ap()
    wgu_io = nc.dram_tensor("wgu", [2 * HF, D], F32, kind="ExternalInput").ap()
    wp_io = nc.dram_tensor("wp", [D, HF], F32, kind="ExternalInput").ap()
    cos_io = nc.dram_tensor("costab", [128, S], F32, kind="ExternalInput").ap()
    sina_io = nc.dram_tensor("sinatab", [128, S], F32, kind="ExternalInput").ap()
    mneg_io = nc.dram_tensor("maskneg", [128, 128], F32, kind="ExternalInput").ap()
    mnegb_io = nc.dram_tensor("masknegbf", [128, 512], BF16, kind="ExternalInput").ap()
    idb_io = nc.dram_tensor("identbf", [128, 128], BF16, kind="ExternalInput").ap()
    idf_io = nc.dram_tensor("identf", [128, 128], F32, kind="ExternalInput").ap()
    y_io = nc.dram_tensor("y", [2, S, D], F32, kind="ExternalOutput").ap()

    sc_wq = nc.dram_tensor("sc_wq", [NK, 128, D], BF16, kind="Internal").ap()
    sc_wo = nc.dram_tensor("sc_wo", [NK, 128, D], BF16, kind="Internal").ap()
    sc_wgu = nc.dram_tensor("sc_wgu", [2 * NHC, 128, D], BF16, kind="Internal").ap()
    sc_wp = nc.dram_tensor("sc_wp", [NK, 128, HF], BF16, kind="Internal").ap()

    ctx = Ctx(nc)

    PS = [nc.alloc_psum_tensor(f"ps{i}", [128, 512], F32) for i in range(4)]
    PTBF = nc.alloc_psum_tensor("ptbf", [128, 2048], BF16)
    PAUX = nc.alloc_psum_tensor("paux", [128, 512], F32)
    PT2 = nc.alloc_psum_tensor("pt2", [128, 512], F32)

    xr = [ctx.sbuf(f"xr{j}", [128, D], F32) for j in range(TPP)]
    xn_fm = ctx.sbuf("xn_fm", [128, TPP * D], BF16)
    h_buf = ctx.sbuf("h_buf", [128, NHC * TS], BF16)
    HB = ctx.offs["h_buf"]
    q_fm = ctx.sbuf("q_fm", [128, HEADS * TS], BF16, at=HB)
    attn_fm = ctx.sbuf("attn_fm", [128, HEADS * TS], BF16, at=HB + HEADS * TS * 2)
    dummy = ctx.sbuf("dummy", [128, D], F32, at=HB + 2 * HEADS * TS * 2)

    k_fm = [ctx.sbuf(f"k_fm{s}", [128, S], BF16) for s in range(2)]
    v_tm = [ctx.sbuf(f"v_tm{s}", [128, S], BF16) for s in range(2)]
    U = ctx.sbuf("U", [128, 2 * S], BF16)
    U2 = ctx.sbuf("U2", [128, 2 * S], BF16)
    UT = ctx.sbuf("UT", [128, 1024], BF16)
    xntm = ctx.sbuf("xntm", [128, 2 * D], BF16)
    cosb = ctx.sbuf("cosb", [128, TW], F32)
    sinab = ctx.sbuf("sinab", [128, TW], F32)
    masknegb = ctx.sbuf("masknegb", [128, 512], BF16)
    idb = ctx.sbuf("identb", [128, 128], BF16)
    idf = ctx.sbuf("identf32", [128, 128], F32)
    ones_bf = ctx.sbuf("ones_bf", [128, 16], BF16)
    wkv_sb = ctx.sbuf("wkv_sb", [128, NK * 256], BF16)
    wqo_buf = ctx.sbuf("wqo_buf", [128, 2 * D], BF16)
    wg_buf = ctx.sbuf("wg_buf", [128, 2 * D], BF16)
    wu_buf = ctx.sbuf("wu_buf", [128, 2 * D], BF16)
    wp_buf = ctx.sbuf("wp_buf", [128, HF // 2], BF16)
    stg = ctx.sbuf("stg", [128, 512], F32)
    stg2 = ctx.sbuf("stg2", [128, 1024], F32)
    hsq = ctx.sbuf("hsq", [128, 1024], BF16)
    sup = ctx.sbuf("sup", [128, 16], F32)
    su = ctx.sbuf("su", [128, 2], F32)
    ru = ctx.sbuf("ru", [128, 2], F32)
    ssb = ctx.sbuf("ssb", [128, TPP], F32)
    inv = ctx.sbuf("inv", [128, TPP], F32)
    stats = ctx.sbuf("stats", [128, 8 * TPP], F32)
    vstg = ctx.sbuf("vstg", [128, 512], BF16)
    cbank = ctx.sbuf("cbank", [128, 8], F32)

    pa = ctx.offs["xr0"]
    p_wf = [ctx.sbuf(f"p_wf{i}", [128, D], F32, at=pa + i * 8192) for i in range(2)]
    p_wb = [ctx.sbuf(f"p_wb{i}", [128, D], BF16, at=pa + 16384 + i * 4096) for i in range(2)]
    p_sg = [ctx.sbuf(f"p_sg{i}", [128, D], BF16, at=pa + 24576 + i * 4096) for i in range(2)]
    p_tq = [ctx.sbuf(f"p_tq{i}", [128, D], BF16, at=HB + i * 4096) for i in range(2)]
    p_zq = [ctx.sbuf(f"p_zq{i}", [128, D], BF16, at=HB + 8192 + i * 4096) for i in range(2)]
    p_wh = [ctx.sbuf(f"p_wh{i}", [128, D], BF16, at=HB + 16384 + i * 4096) for i in range(2)]
    p_ts = [ctx.sbuf(f"p_ts{i}", [128, D], BF16, at=HB + 24576 + i * 4096) for i in range(2)]
    p_gs = [ctx.sbuf(f"p_gs{i}", [128, 32], F32, at=HB + 32768 + i * 160) for i in range(2)]
    p_sh = [ctx.sbuf(f"p_sh{i}", [128, 32], BF16, at=HB + 33120 + i * 64) for i in range(2)]
    p_sf = [ctx.sbuf(f"p_sf{i}", [128, 32], BF16, at=HB + 33248 + i * 64) for i in range(2)]
    p_kvf = ctx.sbuf("p_kvf", [128, NK * 256], F32, at=HB + 33376 + 32)
    p_se = [ctx.sbuf(f"p_se{i}", [128, D], BF16, at=HB + 50176 + i * 4096) for i in range(2)]
    p_zz = [ctx.sbuf(f"p_zz{i}", [128, D], BF16, at=HB + 33408 + i * 4096) for i in range(2)]

    p_wf.append(ctx.sbuf("p_wf2", [128, D], F32, at=ctx.offs["wqo_buf"]))
    p_wb.append(ctx.sbuf("p_wb2", [128, D], BF16, at=ctx.offs["wg_buf"]))
    p_sg.append(ctx.sbuf("p_sg2", [128, D], BF16, at=ctx.offs["wg_buf"] + 4096))
    p_tq.append(ctx.sbuf("p_tq2", [128, D], BF16, at=ctx.offs["wu_buf"]))
    p_zq.append(ctx.sbuf("p_zq2", [128, D], BF16, at=ctx.offs["wu_buf"] + 4096))
    p_zz.append(ctx.sbuf("p_zz2", [128, D], BF16, at=ctx.offs["wp_buf"]))
    p_wh.append(ctx.sbuf("p_wh2", [128, D], BF16, at=ctx.offs["wp_buf"] + 4096))
    p_se.append(ctx.sbuf("p_se2", [128, D], BF16, at=ctx.offs["xn_fm"]))
    p_ts.append(ctx.sbuf("p_ts2", [128, D], BF16, at=ctx.offs["xn_fm"] + 4096))
    p_gs.append(ctx.sbuf("p_gs2", [128, 32], F32, at=ctx.offs["xn_fm"] + 8192))
    p_sf.append(ctx.sbuf("p_sf2", [128, 32], BF16, at=ctx.offs["xn_fm"] + 8352))

    sIN = ctx.sem("sIN")
    sIN2 = ctx.sem("sIN2")
    sXs = [ctx.sem(f"sX{i}") for i in range(4)]
    sWRs = [ctx.sem("sWR0"), ctx.sem("sWR1"), ctx.sem("sWR2")]
    sPs = [ctx.sem("sP0"), ctx.sem("sP1"), ctx.sem("sP2")]
    sWs = [ctx.sem("sW0"), ctx.sem("sW1")]
    sWp = [ctx.sem("sWp0"), ctx.sem("sWp1")]
    sPE = ctx.sem("sPE")
    sSC = ctx.sem("sSC")
    sDV = ctx.sem("sDV")
    sPL = ctx.sem("sPL")
    sOUT = ctx.sem("sOUT")

    def _finish_early0():
        ctx.barrier()
        ctx.dma("scalar", y_io[0, 0:128, :], xr[0][:], sOUT)
        ctx.wait("sync", sOUT)
        ctx.barrier()

    ctx.dma("scalar", masknegb[:], mnegb_io, sIN2)
    ctx.dma("scalar", idb[:], idb_io, sIN2)
    ctx.dma("scalar", idf[:], idf_io, sIN2)
    ctx.dma("scalar", p_kvf[:].rearrange("p (k f) -> p k f", f=256),
            wkv_io.rearrange("(k p) f -> p k f", p=128), sIN2)
    ctx.wait("vector", sIN2)
    ctx.op("vector", lambda e: e.memset(ones_bf[:], 1.0))
    ctx.op("vector", lambda e: e.memset(cbank[:, 0:1], LN_EPS))
    ctx.op("vector", lambda e: e.memset(cbank[:, 1:2], SCALE))
    ctx.op("vector", lambda e: e.memset(cbank[:, 2:3], -EXP_C))
    ctx.op("vector", lambda e: e.memset(cbank[:, 3:4], RMS_EPS))
    ctx.op("vector", lambda e: e.memset(cbank[:, 4:5], 1.0 / HF))
    wkv_mark = ctx.op("vector", lambda e: e.tensor_copy(wkv_sb[:], p_kvf[:]), sem=sDV, fence=True)

    cast_done, write_done = {}, {}
    if stop_stage == "P0":
        _finish_early0()
        return nc

    def prep_attn(w_io, sc, tbase):
        for k in range(NK):
            t = tbase + k
            slot = t % 2
            if t >= 2:
                ctx.wait("sync", sSC, cast_done[t - 2])
            ctx.dma("sync", p_wf[slot][:], w_io[k * 128:(k + 1) * 128, :], sPs[slot])
            ctx.wait("scalar", sPs[slot])
            if t >= 2:
                ctx.wait("scalar", sWRs[slot])
            cast_done[t] = ctx.op(
                "scalar", lambda e, s=slot: e.copy(p_wb[s][:], p_wf[s][:]), sem=sSC)
            ctx.wait("scalar", sSC, cast_done[t])
            for hh in range(2):
                dst = sc[hh * 8:(hh + 1) * 8, :, k * 128:(k + 1) * 128].rearrange("m p f -> p m f")
                srcv = p_wb[slot][:, hh * 1024:(hh + 1) * 1024].rearrange("p (m f) -> p m f", f=128)
                write_done[t] = ctx.dma("scalar", dst, srcv, sWRs[slot])

    prep_attn(wq_io, sc_wq, 0)
    if stop_stage == "P1a":
        _finish_early0()
        return nc
    prep_attn(wo_io, sc_wo, NK)
    ctx.barrier()
    if stop_stage == "P1":
        _finish_early0()
        return nc
    ctx.wait("scalar", sWRs[0])
    ctx.wait("scalar", sWRs[1])
    ctx.wait("sync", sSC)
    ctx.barrier()

    NT_G = 2 * NHC
    NT_P = NK * (HF // D)

    n_ffn = NT_G + NT_P if ffn_tiles is None else ffn_tiles
    m_cast, m_abs, m_red, m_stt1, m_stt2, m_stt3 = {}, {}, {}, {}, {}, {}
    m_T, m_ev, m_tg = {}, {}, {}

    def prep_src(t):
        if t < NT_G:
            return wgu_io[t * 128:(t + 1) * 128, :]
        i = t - NT_G
        kd, hs = i // 4, i % 4
        return wp_io[kd * 128:(kd + 1) * 128, hs * D:(hs + 1) * D]

    def prep_front(t):
        slot = t % 3
        if t >= 3:
            ctx.wait("sync", sSC, m_cast[t - 3])
        ctx.dma("sync", p_wf[slot][:], prep_src(t), sPs[slot])
        ctx.wait("scalar", sPs[slot])
        if t >= 3:
            ctx.wait("scalar", sDV, m_stt3[t - 3])
        m_cast[t] = ctx.op("scalar", lambda e, s=slot: e.copy(p_wb[s][:], p_wf[s][:]),
                           sem=sSC, fence=True)
        ctx.op("scalar", lambda e, s=slot: e.sign(p_sg[s][:], p_wb[s][:]), sem=sSC, fence=True)
        m_abs[t] = ctx.op("scalar", lambda e, s=slot: e.activation(p_tq[s][:], p_wb[s][:], AF.Abs),
                          sem=sSC)
        ctx.wait("vector", sSC, m_cast[t])
        m_red[t] = ctx.op("vector", lambda e, s=slot: e.reduce_sum(
            p_gs[s][:], p_wb[s][:].rearrange("p (g e) -> p g e", e=64), axis=AX.X,
            apply_absolute_value=True), sem=sDV, fence=True)
        m_sf = ctx.op("vector", lambda e, s=slot: e.tensor_scalar(
            p_sf[s][:], p_gs[s][:], 1.0 / 64.0, 1.0e-8, ALU.mult, ALU.max), sem=sDV, fence=True)

    def se_bc(slot):
        return p_sf[slot][:].unsqueeze(-1).broadcast_to([128, 32, 64])

    def prep_quant(t):
        slot = t % 3
        if t <= 1:
            ctx.wait("vector", sDV, wkv_mark)
        ctx.wait("vector", sSC, m_abs[t])
        m_stt1[t] = ctx.op("vector", lambda e, s=slot: e.scalar_tensor_tensor(
            p_zq[s][:].rearrange("p (g e) -> p g e", e=64),
            p_tq[s][:].rearrange("p (g e) -> p g e", e=64), 2.0, se_bc(s),
            ALU.mult, ALU.subtract), sem=sDV, fence=True)
        m_stt2[t] = ctx.op("vector", lambda e, s=slot: e.scalar_tensor_tensor(
            p_zz[s][:].rearrange("p (g e) -> p g e", e=64),
            p_zq[s][:].rearrange("p (g e) -> p g e", e=64), BIGK, se_bc(s),
            ALU.mult, ALU.min), sem=sDV, fence=True)
        if t >= 3:
            ctx.wait("vector", sPE, m_T[t - 3])
        m_stt3[t] = ctx.op("vector", lambda e, s=slot: e.scalar_tensor_tensor(
            p_wh[s][:], p_zz[s][:], 0.0, p_sg[s][:], ALU.max, ALU.mult), sem=sDV)

    def prep_tail(t):
        slot = t % 3
        for g in range(2):
            ctx.wait("scalar", sPE, m_tg[t][g])
            if t >= 3 and g == 0:
                ctx.wait("scalar", sWRs[slot])
            m_ev[t] = ctx.op("scalar", lambda e, s=slot, g=g: e.copy(
                p_ts[s][:, g * 1024:(g + 1) * 1024],
                PTBF[:, g * 1024:(g + 1) * 1024]), sem=sSC)
        if t < NT_G:
            dstw = sc_wgu[t]
        else:
            i = t - NT_G
            kd, hs = i // 4, i % 4
            dstw = sc_wp[kd, :, hs * D:(hs + 1) * D]
        ctx.wait("scalar", sSC, m_ev[t])
        ctx.dma("scalar", dstw, p_ts[slot][:], sWRs[slot])

    def prep_T(t):
        slot = t % 3
        ctx.wait("tensor", sDV, m_stt3[t])
        m_tg[t] = {}
        for g in range(2):
            if t >= 1:
                ctx.wait("tensor", sSC, m_ev.get(t - 1, 0))
            gm = None
            for kk in range(8):
                k = g * 8 + kk
                gm = ctx.op("tensor", lambda e, s=slot, g=g, k=k, kk=kk: e.transpose(
                    PTBF[:, g * 1024 + kk * 128:g * 1024 + (kk + 1) * 128],
                    p_wh[s][:, k * 128:(k + 1) * 128], idb[:]),
                    sem=sPE if kk == 7 else None)
            m_tg[t][g] = gm
        m_T[t] = m_tg[t][1]

    for t in range(n_ffn + 1):
        if t >= 1:
            prep_quant(t - 1)
        if t < n_ffn:
            prep_front(t)
        if t >= 2:
            prep_tail(t - 2)
        if t >= 1:
            prep_T(t - 1)
    if n_ffn >= 1:
        prep_tail(n_ffn - 1)


    if ffn_tiles is not None:
        _finish_early0()
        return nc
    ctx.barrier()
    ctx.wait("sync", sWRs[0])
    ctx.wait("sync", sWRs[1])
    ctx.wait("sync", sWRs[2])
    ctx.wait("sync", sIN)
    ctx.wait("sync", sIN2)
    ctx.barrier()

    def _finish_early():
        ctx.dma("scalar", y_io[0, 0:128, :], xr[0][:], sOUT)
        ctx.wait("sync", sOUT)
        ctx.barrier()

    if stop_stage == "P":
        _finish_early()
        return nc

    def fm_rhs(k):
        v = xn_fm[:].rearrange("p (j k f) -> p k j f", k=NK, f=128)
        return v[:, k]

    ptbf_g = {0: None, 1: None}

    def wait_ptbf(eng, half):
        m = ptbf_g[half]
        if m is not None:
            ctx.wait(eng, m[0], m[1])

    def pe_warm(sem, v):
        ctx.wait("tensor", sem, v)
        ctx.op("tensor", lambda e: e.transpose(PT2[:, 0:128], idf[:], idf[:]))

    def ln_block(src_tiles, pt2_guard=None):
        if pt2_guard is not None:
            ctx.wait("tensor", sDV, pt2_guard)
        stm = {}
        for j in range(TPP):
            xt = src_tiles[j]
            st = stats[:, j * 8:(j + 1) * 8]
            ctx.wait("scalar", sXs[j])
            ctx.op("scalar", lambda e, x=xt, st=st: e.activation(
                dummy[:], x[:], AF.Identity, accum_out=st[:, 0:1]), sem=sSC, fence=True)
            stm[j] = ctx.op("scalar", lambda e, x=xt, st=st: e.activation(
                dummy[:], x[:], AF.Square, accum_out=st[:, 1:2]), sem=sSC, fence=True)
            if pt2_guard is not None:
                pe_warm(sSC, stm[j])
        tpd = {}
        ev_half = {0: 0, 1: 0}
        for j in range(TPP):
            xt = src_tiles[j]
            st = stats[:, j * 8:(j + 1) * 8]
            xsl = j % 2
            ctx.wait("vector", sSC, stm[j])
            ctx.op("vector", lambda e, st=st: e.tensor_scalar(
                st[:, 2:3], st[:, 0:1], 1.0 / D, None, ALU.mult), fence=True)
            ctx.op("vector", lambda e, st=st: e.tensor_scalar(
                st[:, 3:4], st[:, 1:2], 1.0 / D, None, ALU.mult), fence=True)
            ctx.op("vector", lambda e, st=st: e.tensor_tensor(
                st[:, 4:5], st[:, 2:3], st[:, 2:3], ALU.mult), fence=True)
            v1 = ctx.op("vector", lambda e, st=st: e.tensor_sub(
                st[:, 5:6], st[:, 3:4], st[:, 4:5]), sem=sDV, fence=True)
            if pt2_guard is not None:
                pe_warm(sDV, v1)
            ctx.wait("scalar", sDV, v1)
            s1 = ctx.op("scalar", lambda e, st=st: e.activation(
                st[:, 6:7], st[:, 5:6], AF.Sqrt, bias=cbank[:, 0:1]), sem=sSC)
            ctx.wait("vector", sSC, s1)
            ctx.op("vector", lambda e, st=st: e.reciprocal(st[:, 7:8], st[:, 6:7]), fence=True)
            v2 = ctx.op("vector", lambda e, st=st: e.scalar_tensor_tensor(
                st[:, 4:5], st[:, 2:3], -1.0, st[:, 7:8], ALU.mult, ALU.mult), sem=sDV,
                fence=True)
            if pt2_guard is not None:
                pe_warm(sDV, v2)
            ctx.wait("scalar", sDV, v2)
            if j >= 2:
                ctx.wait("scalar", sPE, tpd[j - 2])
            s2 = ctx.op("scalar", lambda e, x=xt, st=st, xs=xsl: e.activation(
                xntm[:, xs * D:(xs + 1) * D], x[:], AF.Identity,
                scale=st[:, 7:8], bias=st[:, 4:5]), sem=sSC)
            ctx.wait("tensor", sSC, s2)
            for g in range(2):
                wait_ptbf("tensor", g)
                for kk in range(8):
                    k = g * 8 + kk
                    tpd[j] = ctx.op("tensor", lambda e, g=g, k=k, kk=kk, xs=xsl: e.transpose(
                        PTBF[:, g * 1024 + kk * 128:g * 1024 + (kk + 1) * 128],
                        xntm[:, xs * D + k * 128:xs * D + (k + 1) * 128], idb[:]),
                        sem=sPE if kk == 7 else None)
                ctx.wait("vector", sPE, tpd[j])
                ev_half[g] = ctx.op("vector", lambda e, g=g, j=j: e.tensor_copy(
                    xn_fm[:, j * D + g * 1024:j * D + (g + 1) * 1024],
                    PTBF[:, g * 1024:(g + 1) * 1024]), sem=sDV)
                ptbf_g[g] = (sDV, ev_half[g])
        return max(ev_half[0], ev_half[1])

    out_count = {}
    prev_i_add = None
    for p in range(NPASS):
        tlist = [(s, p * TPS + jj) for s in range(2) for jj in range(TPS)]

        if p >= 1:
            ctx.wait("sync", sOUT, out_count[p - 1])
        for j, (s, ti) in enumerate(tlist):
            ctx.dma("sync", xr[j][:], x_io[s, ti * 128:(ti + 1) * 128, :], sXs[j])
        c0 = p * TW
        ctx.dma("scalar", cosb[:], cos_io[:, c0:c0 + TW], sIN2)
        ctx.dma("scalar", sinab[:], sina_io[:, c0:c0 + TW], sIN2)
        lnm1 = ln_block(xr, pt2_guard=prev_i_add)

        if stop_stage == "A":
            _finish_early()
            return nc
        ctx.wait("tensor", sDV, lnm1)
        for k in range(NK):
            ctx.op("tensor", lambda e, k=k, r=fm_rhs(k): e.matmul(
                PS[0][:], wkv_sb[:, k * 256:k * 256 + 128], r,
                start=(k == 0), stop=(k == NK - 1)),
                sem=sPE if k == NK - 1 else None)
        k_pe = ctx.val(sPE)
        for k in range(NK):
            ctx.op("tensor", lambda e, k=k, r=fm_rhs(k): e.matmul(
                PS[1][:], wkv_sb[:, k * 256 + 128:k * 256 + 256], r,
                start=(k == 0), stop=(k == NK - 1)),
                sem=sPE if k == NK - 1 else None)
        v_pe = ctx.val(sPE)
        ctx.wait("vector", sIN2)
        ctx.wait("vector", sPE, k_pe)
        for s in range(2):
            b = s * TW
            dstk = k_fm[s][:, c0:c0 + TW]
            ctx.op("vector", lambda e, b=b: e.tensor_mul(
                stg[:, 0:TW], PS[0][:, b:b + TW], cosb[:]), fence=True)
            ctx.op("vector", lambda e, b=b: e.tensor_mul(
                stg2[0:64, 0:TW], PS[0][64:128, b:b + TW], sinab[0:64, :]), fence=True)
            ctx.op("vector", lambda e, b=b: e.tensor_mul(
                stg2[64:128, 0:TW], PS[0][0:64, b:b + TW], sinab[64:128, :]), fence=True)
            krope_mark = ctx.op("vector", lambda e, d=dstk: e.tensor_add(
                d, stg[:, 0:TW], stg2[:, 0:TW]), sem=sDV, fence=True)
        ctx.wait("scalar", sPE, v_pe)
        vc = ctx.op("scalar", lambda e: e.copy(vstg[:, 0:TS], PS[1][:, 0:TS]), sem=sSC)
        ctx.wait("tensor", sSC, vc)
        vtd = {}
        for j, (s, ti) in enumerate(tlist):
            half = j % 2
            wait_ptbf("tensor", half)
            tmark = ctx.op("tensor", lambda e, h=half, j=j: e.transpose(
                PTBF[:, h * 1024:h * 1024 + 128], vstg[:, j * 128:(j + 1) * 128], idb[:]), sem=sPE)
            ctx.wait("vector", sPE, tmark)
            vtd[j] = ctx.op("vector", lambda e, h=half, s=s, ti=ti: e.tensor_copy(
                v_tm[s][:, ti * 128:(ti + 1) * 128], PTBF[:, h * 1024:h * 1024 + 128]), sem=sDV)
            ptbf_g[half] = (sDV, vtd[j])
        b_krope = krope_mark

        if stop_stage == "B":
            _finish_early()
            return nc
        ctx.dma("sync", wqo_buf[:, 0:D], sc_wq[0], sWs[0])
        ctx.dma("sync", wqo_buf[:, D:2 * D], sc_wq[1], sWs[1])
        rope_done = {}
        for mi in range(HEADS):
            slot = mi % 2
            ctx.wait("tensor", sWs[slot])
            if mi == 0:
                ctx.wait("tensor", sDV, b_krope)
            if mi == 1:
                ctx.wait("tensor", sSC, vc)
            if mi >= 2:
                ctx.wait("tensor", sDV, rope_done[mi - 2])
            mmv = None
            for k in range(NK):
                mmv = ctx.op("tensor", lambda e, sl=slot, k=k, r=fm_rhs(k): e.matmul(
                    PS[sl][:], wqo_buf[:, sl * D + k * 128:sl * D + (k + 1) * 128], r,
                    start=(k == 0), stop=(k == NK - 1)),
                    sem=sPE if k == NK - 1 else None)
            if mi + 2 < HEADS:
                ctx.wait("sync", sPE, mmv)
                ctx.dma("sync", wqo_buf[:, slot * D:(slot + 1) * D], sc_wq[mi + 2], sWs[slot])
            ctx.wait("vector", sPE, mmv)
            for s in range(2):
                b = s * TW
                ctx.op("vector", lambda e, sl=slot, b=b: e.tensor_mul(
                    stg[:, 0:TW], PS[sl][:, b:b + TW], cosb[:]), fence=True)
                ctx.op("vector", lambda e, sl=slot, b=b: e.tensor_mul(
                    stg2[0:64, 0:TW], PS[sl][64:128, b:b + TW], sinab[0:64, :]), fence=True)
                ctx.op("vector", lambda e, sl=slot, b=b: e.tensor_mul(
                    stg2[64:128, 0:TW], PS[sl][0:64, b:b + TW], sinab[64:128, :]), fence=True)
                rope_done[mi] = ctx.op("vector", lambda e, mi=mi, b=b: e.tensor_add(
                    q_fm[:, mi * TS + b:mi * TS + b + TW], stg[:, 0:TW], stg2[:, 0:TW]),
                    sem=sDV, fence=True)
        c_mm_last = ctx.val(sPE)

        if stop_stage == "C":
            _finish_early()
            return nc
        NU = len(tlist) * HEADS
        exp_last, un_mark, t_mark, av_mark, aev_mark = {}, {}, {}, {}, {}
        ut_av = {0: 0, 1: 0}
        ps_exp = {0: (sDV, rope_done[HEADS - 2]), 1: (sDV, rope_done[HEADS - 1]),
                  2: None, 3: None}
        gctr = [0]
        sctr = [2]

        def d_front(u):
            j, h = u // HEADS, u % HEADS
            s, ti = tlist[j]
            nk = ti + 1
            ngrp = (nk + 3) // 4
            par = u % 2
            qh = q_fm[:, h * TS + j * 128:h * TS + (j + 1) * 128]
            ctx.wait("tensor", sDV, rope_done[h])
            for g in range(ngrp):
                w = min(4, nk - 4 * g)
                bank = sctr[0] % 4
                sctr[0] += 1
                has_diag = (4 * g <= nk - 1 < 4 * g + w)
                if ps_exp[bank] is not None:
                    ctx.wait("tensor", ps_exp[bank][0], ps_exp[bank][1])
                mm = ctx.op("tensor", lambda e, b=bank, g=g, w=w, qh=qh, s=s: e.matmul(
                    PS[b][:, 0:w * 128], qh, k_fm[s][:, g * 512:g * 512 + w * 128],
                    start=True, stop=not has_diag),
                    sem=None if has_diag else sPE)
                if has_diag:
                    mm = ctx.op("tensor", lambda e, b=bank, w=w: e.matmul(
                        PS[b][:, 0:w * 128], idb[:], masknegb[:, (4 - w) * 128:512],
                        start=False, stop=True), sem=sPE)
                ctx.wait("scalar", sPE, mm)
                if g == 0 and u >= 2:
                    ctx.wait("scalar", sDV, un_mark[u - 2])
                exp_last[u] = ctx.op("scalar", lambda e, b=bank, g=g, w=w, par=par: e.activation(
                    U[:, par * S + g * 512:par * S + g * 512 + w * 128],
                    PS[b][:, 0:w * 128], AF.Exp,
                    bias=cbank[:, 2:3], scale=cbank[:, 1:2],
                    accum_out=sup[:, par * 8 + g:par * 8 + g + 1]), sem=sSC)
                ps_exp[bank] = (sSC, exp_last[u])

        def d_tail(u):
            j, h = u // HEADS, u % HEADS
            s, ti = tlist[j]
            nk = ti + 1
            ngrp = (nk + 3) // 4
            par = u % 2
            ctx.wait("vector", sSC, exp_last[u])
            ctx.op("vector", lambda e, par=par, n=ngrp: e.reduce_sum(
                su[:, par:par + 1], sup[:, par * 8:par * 8 + n], axis=AX.X), fence=True)
            ctx.op("vector", lambda e, par=par: e.reciprocal(
                ru[:, par:par + 1], su[:, par:par + 1]), fence=True)
            if u >= 2:
                ctx.wait("vector", sPE, t_mark[u - 2])
            un_mark[u] = ctx.op("vector", lambda e, par=par, nk=nk: e.tensor_scalar(
                U2[:, par * S:par * S + nk * 128], U[:, par * S:par * S + nk * 128],
                ru[:, par:par + 1], None, ALU.mult), sem=sDV)
            ctx.wait("tensor", sDV, un_mark[u])
            if u >= 2:
                ctx.wait("tensor", sSC, aev_mark[u - 2])
            for g in range(ngrp):
                w = min(4, nk - 4 * g)
                slot = gctr[0] % 2
                wait_ptbf("tensor", slot)
                tm = None
                for kk in range(w):
                    kt = g * 4 + kk
                    tm = ctx.op("tensor", lambda e, sl=slot, kt=kt, kk=kk, par=par: e.transpose(
                        PTBF[:, sl * 1024 + kk * 128:sl * 1024 + (kk + 1) * 128],
                        U2[:, par * S + kt * 128:par * S + (kt + 1) * 128], idb[:]),
                        sem=sPE if kk == w - 1 else None)
                t_mark[u] = tm
                eeng = "scalar" if gctr[0] % 2 == 0 else "vector"
                esem = sSC if gctr[0] % 2 == 0 else sDV
                ctx.wait(eeng, sPE, tm)
                ctx.wait(eeng, sPE, ut_av[slot])
                ev = ctx.op(eeng, lambda e, sl=slot, w=w: (
                    e.copy if hasattr(e, "copy") else e.tensor_copy)(
                    UT[:, sl * 512:sl * 512 + w * 128],
                    PTBF[:, sl * 1024:sl * 1024 + w * 128]), sem=esem)
                ptbf_g[slot] = (esem, ev)
                ctx.wait("tensor", esem, ev)
                for kk in range(w):
                    kt = g * 4 + kk
                    av_mark[u] = ctx.op("tensor", lambda e, sl=slot, kk=kk, kt=kt, nk=nk, s=s, par=par: e.matmul(
                        PAUX[:, par * 128:(par + 1) * 128],
                        v_tm[s][:, kt * 128:(kt + 1) * 128],
                        UT[:, sl * 512 + kk * 128:sl * 512 + (kk + 1) * 128],
                        start=(kt == 0), stop=(kt == nk - 1), skip_group_check=True),
                        sem=sPE if kk == w - 1 else None)
                ut_av[slot] = av_mark[u]
                gctr[0] += 1
            ctx.wait("scalar", sPE, av_mark[u])
            aev_mark[u] = ctx.op("scalar", lambda e, h=h, j=j, par=par: e.copy(
                attn_fm[:, h * TS + j * 128:h * TS + (j + 1) * 128],
                PAUX[:, par * 128:(par + 1) * 128]), sem=sSC)

        for u in range(NU + 1):
            if u < NU:
                d_front(u)
            if u >= 1:
                d_tail(u - 1)
        d_ps_exp = ps_exp
        aev_last = aev_mark[NU - 1]

        if stop_stage == "D":
            _finish_early()
            return nc
        ctx.wait("sync", sPE, c_mm_last)
        ctx.dma("sync", wqo_buf[:, 0:D], sc_wo[0], sWs[0])
        ctx.dma("sync", wqo_buf[:, D:2 * D], sc_wo[1], sWs[1])
        add_done, ev_done, tp_done = {}, {}, {}
        for mo in range(NK + 1):
            slot = mo % 2
            mmv = None
            if mo < NK:
                ctx.wait("tensor", sWs[slot])
                if mo == 0:
                    ctx.wait("tensor", sSC, aev_last)
                if mo < 2:
                    m = d_ps_exp[mo]
                    if m is not None:
                        ctx.wait("tensor", m[0], m[1])
                if mo >= 2:
                    ctx.wait("tensor", sSC, ev_done[mo - 2])
                for ki in range(NK):
                    mmv = ctx.op("tensor", lambda e, sl=slot, ki=ki: e.matmul(
                        PS[sl][:], wqo_buf[:, sl * D + ki * 128:sl * D + (ki + 1) * 128],
                        attn_fm[:, ki * TS:(ki + 1) * TS],
                        start=(ki == 0), stop=(ki == NK - 1)),
                        sem=sPE if ki == NK - 1 else None)
            if mo >= 1:
                pc = mo - 1
                pslot = pc % 2
                pps = PT2 if pslot == 0 else PAUX
                ctx.wait("tensor", sSC, ev_done[pc])
                if pc >= 2:
                    ctx.wait("tensor", sDV, add_done[pc - 2])
                for j in range(TPP):
                    tp_done[pc] = ctx.op("tensor", lambda e, j=j, ps2=pps, sl2=pslot: e.transpose(
                        ps2[:, j * 128:(j + 1) * 128],
                        stg2[:, sl2 * 512 + j * 128:sl2 * 512 + (j + 1) * 128], idf[:]),
                        sem=sPE if j == TPP - 1 else None)
                ctx.wait("vector", sPE, tp_done[pc])
                for j in range(TPP):
                    add_done[pc] = ctx.op("vector", lambda e, j=j, pc=pc, ps2=pps: e.tensor_add(
                        xr[j][:, pc * 128:(pc + 1) * 128], xr[j][:, pc * 128:(pc + 1) * 128],
                        ps2[:, j * 128:(j + 1) * 128]),
                        sem=sDV if j == TPP - 1 else None)
            if mo >= NK:
                break
            if mo + 2 < NK:
                ctx.wait("sync", sPE, mmv)
                ctx.dma("sync", wqo_buf[:, slot * D:(slot + 1) * D], sc_wo[mo + 2], sWs[slot])
            ctx.wait("scalar", sPE, mmv)
            if mo >= 2:
                ctx.wait("scalar", sDV, add_done[mo - 2])
            ev_done[mo] = ctx.op("scalar", lambda e, sl=slot: e.copy(
                stg2[:, sl * 512:(sl + 1) * 512], PS[sl][:]), sem=sSC)
        e_mm_last = ctx.val(sPE)
        e_ev_done = ev_done
        e_add_last = add_done[NK - 1]

        if stop_stage == "E":
            _finish_early()
            return nc
        ctx.wait("scalar", sDV, e_add_last)
        lnm2 = ln_block(xr, pt2_guard=e_add_last)

        if stop_stage == "F":
            _finish_early()
            return nc
        ctx.wait("sync", sPE, e_mm_last)
        for hc in range(2):
            ctx.dma("sync", wg_buf[:, (hc % 2) * D:(hc % 2 + 1) * D], sc_wgu[hc], sWs[hc % 2])
            ctx.dma("sync", wu_buf[:, (hc % 2) * D:(hc % 2 + 1) * D], sc_wgu[NHC + hc], sWs[hc % 2])
        HH = HF // 4
        ctx.dma("sync", wp_buf[:, 0:HH], sc_wp[0, :, 0:HH], sWp[0])
        ctx.dma("sync", wp_buf[:, HH:2 * HH], sc_wp[0, :, HH:2 * HH], sWp[1])
        hv_done, ss_done, h2_done, g_sl1 = {}, {}, {}, {}
        for hc in range(NHC + 1):
            slot = hc % 2
            ps_g, ps_u = PS[2 * slot], PS[2 * slot + 1]
            mmv = None
            if hc < NHC:
                ctx.wait("tensor", sWs[slot])
                if hc == 0:
                    ctx.wait("tensor", sDV, lnm2)
                    ctx.wait("tensor", sSC, e_ev_done[NK - 1])
                if hc == 1:
                    for m in (d_ps_exp[2], d_ps_exp[3]):
                        if m is not None:
                            ctx.wait("tensor", m[0], m[1])
                if hc >= 2:
                    ctx.wait("tensor", sDV, hv_done[hc - 2])
                for k in range(NK):
                    ctx.op("tensor", lambda e, pg=ps_g, sl=slot, k=k, r=fm_rhs(k): e.matmul(
                        pg[:], wg_buf[:, sl * D + k * 128:sl * D + (k + 1) * 128], r,
                        start=(k == 0), stop=(k == NK - 1)))
                for k in range(NK):
                    mmv = ctx.op("tensor", lambda e, pu=ps_u, sl=slot, k=k, r=fm_rhs(k): e.matmul(
                        pu[:], wu_buf[:, sl * D + k * 128:sl * D + (k + 1) * 128], r,
                        start=(k == 0), stop=(k == NK - 1)),
                        sem=sPE if k == NK - 1 else None)
            if hc >= 1:
                pc = hc - 1
                if pc == 0:
                    ctx.wait("tensor", sDV, e_add_last)
                ctx.wait("tensor", sSC, h2_done[pc])
                for j in range(TPP):
                    ss_done[pc] = ctx.op("tensor", lambda e, j=j, pc=pc, hq=pc % 2: e.matmul(
                        PAUX[:, 128 + j:129 + j],
                        hsq[:, hq * 512 + j * 128:hq * 512 + (j + 1) * 128],
                        ones_bf[:, 0:1],
                        start=(pc == 0), stop=(pc == NHC - 1), skip_group_check=True),
                        sem=sPE if j == TPP - 1 else None)
            if hc >= NHC:
                break
            if hc + 2 < NHC:
                ctx.wait("sync", sPE, mmv)
                ctx.dma("sync", wg_buf[:, slot * D:(slot + 1) * D], sc_wgu[hc + 2], sWs[slot])
                ctx.dma("sync", wu_buf[:, slot * D:(slot + 1) * D], sc_wgu[NHC + hc + 2], sWs[slot])
            ctx.wait("scalar", sPE, mmv)
            if hc >= 2:
                ctx.wait("scalar", sDV, hv_done[hc - 2])
            sl1 = ctx.op("scalar", lambda e, pg=ps_g, sl=slot: e.activation(
                stg[:, 0:512] if sl == 0 else stg2[:, 0:512], pg[:], AF.Silu), sem=sSC)
            g_sl1[hc] = sl1
            ctx.wait("vector", sSC, sl1)
            hv_done[hc] = ctx.op("vector", lambda e, pu=ps_u, hc=hc, sl=slot: e.tensor_mul(
                h_buf[:, hc * TS:(hc + 1) * TS],
                stg[:, 0:512] if sl == 0 else stg2[:, 0:512], pu[:]), sem=sDV)
            ctx.wait("scalar", sDV, hv_done[hc])
            if hc >= 2:
                ctx.wait("scalar", sPE, ss_done[hc - 2])
            h2_done[hc] = ctx.op("scalar", lambda e, hc=hc, hq=slot: e.activation(
                hsq[:, hq * 512:(hq + 1) * 512], h_buf[:, hc * TS:(hc + 1) * TS],
                AF.Square), sem=sSC)
        g_ss_last = ss_done[NHC - 1]
        g_hv = hv_done

        if stop_stage == "G":
            _finish_early()
            return nc
        ctx.wait("vector", sPE, g_ss_last)
        hv = ctx.op("vector", lambda e: e.tensor_copy(ssb[:], PAUX[:, 128:128 + TPP]), sem=sDV)
        ctx.wait("scalar", sDV, hv)
        sh = ctx.op("scalar", lambda e: e.activation(
            stats[:, 0:TPP], ssb[:], AF.Sqrt, scale=cbank[:, 4:5], bias=cbank[:, 3:4]), sem=sSC)
        ctx.wait("vector", sSC, sh)
        inv_mark = ctx.op("vector", lambda e: e.reciprocal(inv[:], stats[:, 0:TPP]), sem=sDV)
        h_hv_mark = hv

        if stop_stage == "H":
            _finish_early()
            return nc
        NHH = NHC // 4
        NST = 4
        add_done, ev_done, tp_done = {}, {}, {}
        for mo in range(NK + 1):
            slot = mo % 2
            mmv = None
            if mo < NK:
                if mo >= 2:
                    ctx.wait("tensor", sSC, ev_done[mo - 2])
                if mo == 0:
                    ctx.wait("tensor", sSC, g_sl1[NHC - 2])
                    ctx.wait("tensor", sDV, g_hv[NHC - 2])
                if mo == 1:
                    ctx.wait("tensor", sSC, g_sl1[NHC - 1])
                    ctx.wait("tensor", sDV, g_hv[NHC - 1])
                for h in range(NST):
                    step = NST * mo + h
                    bslot = step % 2
                    ctx.wait("tensor", sWp[bslot])
                    for hk_l in range(NHH):
                        hk = h * NHH + hk_l
                        if mo == 0:
                            ctx.wait("tensor", sDV, g_hv[hk])
                        mmv = ctx.op("tensor", lambda e, sl=slot, bs=bslot, hk=hk, hk_l=hk_l: e.matmul(
                            PS[sl][:], wp_buf[:, bs * HH + hk_l * 128:bs * HH + (hk_l + 1) * 128],
                            h_buf[:, hk * TS:(hk + 1) * TS],
                            start=(hk == 0), stop=(hk == NHC - 1)),
                            sem=sPE if hk_l == NHH - 1 else None)
                    nstep = step + 2
                    if nstep < NST * NK:
                        nmo, nh = nstep // NST, nstep % NST
                        ctx.wait("sync", sPE, mmv)
                        ctx.dma("sync", wp_buf[:, bslot * HH:(bslot + 1) * HH],
                                sc_wp[nmo, :, nh * HH:(nh + 1) * HH], sWp[bslot])
            if mo >= 1:
                pc = mo - 1
                pslot = pc % 2
                pps = PT2 if pslot == 0 else PAUX
                ctx.wait("tensor", sSC, ev_done[pc])
                if pc == 0:
                    ctx.wait("tensor", sDV, e_add_last)
                if pc == 1:
                    ctx.wait("tensor", sDV, h_hv_mark)
                if pc >= 2:
                    ctx.wait("tensor", sDV, add_done[pc - 2])
                for j in range(TPP):
                    tp_done[pc] = ctx.op("tensor", lambda e, j=j, ps2=pps, sl2=pslot: e.transpose(
                        ps2[:, j * 128:(j + 1) * 128],
                        stg2[:, sl2 * 512 + j * 128:sl2 * 512 + (j + 1) * 128], idf[:]),
                        sem=sPE if j == TPP - 1 else None)
                ctx.wait("vector", sPE, tp_done[pc])
                if pc == 0:
                    ctx.wait("vector", sDV, inv_mark)
                for j in range(TPP):
                    add_done[pc] = ctx.op("vector", lambda e, j=j, pc=pc, ps2=pps: e.scalar_tensor_tensor(
                        xr[j][:, pc * 128:(pc + 1) * 128],
                        ps2[:, j * 128:(j + 1) * 128], inv[:, j:j + 1],
                        xr[j][:, pc * 128:(pc + 1) * 128],
                        ALU.mult, ALU.add),
                        sem=sDV if j == TPP - 1 else None)
            if mo >= NK:
                break
            ctx.wait("scalar", sPE, mmv)
            if mo >= 2:
                ctx.wait("scalar", sDV, add_done[mo - 2])
            ev_done[mo] = ctx.op("scalar", lambda e, sl=slot: e.copy(
                stg2[:, sl * 512:(sl + 1) * 512], PS[sl][:]), sem=sSC)
        i_add_last = add_done[NK - 1]
        prev_i_add = i_add_last

        if stop_stage == "I":
            _finish_early()
            return nc
        ctx.wait("scalar", sDV, i_add_last)
        for j, (s, ti) in enumerate(tlist):
            ctx.dma("scalar", y_io[s, ti * 128:(ti + 1) * 128, :], xr[j][:], sOUT)
        out_count[p] = ctx.val(sOUT)
        ctx.barrier()

    ctx.wait("sync", sOUT)
    ctx.barrier()
    return nc


_CACHE = {}


def _get_nc(S):
    if S not in _CACHE:
        _CACHE[S] = build_nc(S)
    return _CACHE[S]


def _tables(S):
    inv_freq = 1.0 / (10000.0 ** (np.arange(0, DH, 2, dtype=np.float64) / DH))
    pos = np.arange(S, dtype=np.float64)[:, None] * inv_freq[None, :]
    cos = np.cos(pos)
    sin = np.sin(pos)
    cos_t = np.concatenate([cos, cos], axis=1).T.astype(np.float32)
    sina_t = np.concatenate([-sin, sin], axis=1).T.astype(np.float32)
    return np.ascontiguousarray(cos_t), np.ascontiguousarray(sina_t)


def kernel(L0_attn_x, L0_attn_Wq, L0_attn_Wkv, L0_attn_Wout, L0_ffn_w_gate_up, L0_ffn_w_proj):
    x = np.ascontiguousarray(np.asarray(L0_attn_x, dtype=np.float32))
    wq = np.ascontiguousarray(np.asarray(L0_attn_Wq, dtype=np.float32))
    wkv = np.ascontiguousarray(np.asarray(L0_attn_Wkv, dtype=np.float32))
    wo = np.ascontiguousarray(np.asarray(L0_attn_Wout, dtype=np.float32))
    wgu = np.ascontiguousarray(np.asarray(L0_ffn_w_gate_up, dtype=np.float32))
    wp = np.ascontiguousarray(np.asarray(L0_ffn_w_proj, dtype=np.float32))
    B, S, _ = x.shape
    n_cores = 8
    per = B // n_cores
    cos_t, sina_t = _tables(S)
    mneg_np = np.triu(np.full((128, 128), MASK_NEG, dtype=np.float32), 1)
    mnegb_np = np.zeros((128, 512), dtype=BFNP)
    mnegb_np[:, 384:512] = mneg_np.astype(BFNP)
    idb_np = np.eye(128, dtype=BFNP)
    idf_np = np.eye(128, dtype=np.float32)
    nc = _get_nc(S)
    in_maps = []
    for c in range(n_cores):
        in_maps.append({
            "x": np.ascontiguousarray(x[c * per:(c + 1) * per]),
            "wq": wq, "wkv": wkv, "wo": wo, "wgu": wgu, "wp": wp,
            "costab": cos_t, "sinatab": sina_t,
            "maskneg": mneg_np, "masknegbf": mnegb_np, "identbf": idb_np, "identf": idf_np,
        })
    res = run_bass_kernel_spmd(nc, in_maps, core_ids=list(range(n_cores)))
    out = np.concatenate([res.results[c]["y"] for c in range(n_cores)], axis=0)
    return np.ascontiguousarray(out.astype(np.float32))
